# version 16
# baseline (speedup 1.0000x reference)
import math
import numpy as np
import concourse.bass as bass
import concourse.mybir as mybir
from concourse.bass_utils import run_bass_kernel_spmd

F32 = mybir.dt.float32
BF16 = mybir.dt.bfloat16
AF = mybir.ActivationFunctionType
ALU = mybir.AluOpType
AX = mybir.AxisListType

D = 1024
NCH = 8
NH = 16
NKV = 4
HD = 64
B, L = 4, 4096
CORE_T = 2048
HALO = 16
TILE = 512
NT = CORE_T // TILE
TT = TILE + HALO
SUBTILES = [(0, 112), (112, 112), (224, 112), (336, 112), (448, 64)]
VGROUPS = [(0, 128), (112, 128), (224, 128), (336, 128), (448, 80)]
NEG = -1e30
SHIFT = 8.0


def _slopes():
    return np.array([2.0 ** (-0.5 * (h + 1)) for h in range(NH)], dtype=np.float64)


def _mask_tile(qlen, first):
    kw = qlen + 16
    sl = _slopes()
    m = np.full((qlen, NH, kw), NEG, dtype=np.float64)
    for i in range(qlen):
        for j in range(i, min(kw, i + 17)):
            if first and j < 16:
                continue
            m[i, :, j] = sl * (j - i - 16) - SHIFT
    m = m[:, [2 * (s % 8) + s // 8 for s in range(NH)], :]
    return m.astype(np.float32).reshape(qlen, NH * kw)


def _consts():
    c = {}
    c["ident"] = np.eye(128, dtype=np.float32)
    c["logm"] = _mask_tile(112, False)
    c["logmf"] = _mask_tile(112, True)
    c["logm64"] = _mask_tile(64, False)
    selq = np.zeros((128, NCH * NH), dtype=np.float32)
    selqT = np.zeros((NH, NCH * 128), dtype=np.float32)
    for ci in range(NCH):
        for p in range(128):
            h = 2 * ci + (1 if p >= 64 else 0)
            selq[p, 16 * ci + h] = 1.0 / 64.0
            selqT[h, 128 * ci + p] = 1.0
    c["selq"], c["selqT"] = selq, selqT
    selk = np.zeros((128, 2 * NKV), dtype=np.float32)
    selkT = np.zeros((NKV, 2 * 128), dtype=np.float32)
    for ci in range(2):
        for p in range(128):
            kv = 2 * ci + (1 if p >= 64 else 0)
            selk[p, 4 * ci + kv] = 1.0 / 64.0
            selkT[kv, 128 * ci + p] = 1.0
    c["selk"], c["selkT"] = selk, selkT
    return c


_DBG_STOP = [None]


class _StopBuild(Exception):
    pass


def _chk(tt, tag):
    st = _DBG_STOP[0]
    if st is None:
        return
    stile, stag = st
    if tt > stile or (tt == stile and tag == stag):
        raise _StopBuild()


class _Sem:
    def __init__(self, nc, name):
        self.h = nc.semaphore(name).__enter__()
        self.n = 0

    def inc(self, inst, k=1):
        inst.then_inc(self.h, k)
        self.n += k


class _DefInst:
    __slots__ = ("incs",)

    def __init__(self):
        self.incs = []

    def then_inc(self, sem, k):
        self.incs.append((sem, k))
        return self


def _ap_span(ap):
    try:
        t = ap.tensor.name
        pairs = [list(p) for p in ap.ap]
        bp = ap.base_partition
        if callable(bp):
            bp = bp()
        bp = bp or 0
        np_ = pairs[0][1] if pairs else 128
        lo = ap.offset
        hi = ap.offset
        for st, cnt in pairs[1:]:
            if st > 0:
                hi += st * (cnt - 1)
            elif st < 0:
                lo += st * (cnt - 1)
        return (t, bp, bp + np_, lo, hi + 1)
    except Exception:
        return None


def _overlap(s1, s2):
    return (s1[0] == s2[0] and s1[1] < s2[2] and s2[1] < s1[2]
            and s1[3] < s2[4] and s2[3] < s1[4])


class _EngRec:

    _HAZARD = ("vector", "scalar", "gpsimd")

    def __init__(self, name):
        self._name = name
        self._ops = []
        self._wr = []
        self._rd = []

    def __getattr__(self, attr):
        if attr.startswith("_"):
            raise AttributeError(attr)
        rec = self

        def call(*a, **kw):
            if rec._name in _EngRec._HAZARD and attr not in ("wait_ge", "drain"):
                wsp, rsp = [], []
                args = list(a)
                for i, arg in enumerate(args):
                    sp = _ap_span(arg) if hasattr(arg, "tensor") else None
                    if sp is None:
                        continue
                    (wsp if i == 0 else rsp).append(sp)
                for k2, arg in kw.items():
                    sp = _ap_span(arg) if hasattr(arg, "tensor") else None
                    if sp is None:
                        continue
                    (wsp if k2 == "accum_out" else rsp).append(sp)
                conflict = any(_overlap(s, w) for s in wsp + rsp for w in rec._wr)                     or any(_overlap(s, r) for s in wsp for r in rec._rd)
                if conflict:
                    rec._ops.append(("drain", (), {}, _DefInst()))
                    rec._wr, rec._rd = [], []
                rec._wr.extend(wsp)
                rec._rd.extend(rsp)
            di = _DefInst()
            rec._ops.append((attr, a, kw, di))
            return di
        return call

    def _flush(self, eng):
        for attr, a, kw, di in self._ops:
            inst = getattr(eng, attr)(*a, **kw)
            for sem, k in di.incs:
                inst.then_inc(sem, k)


def build_program():
    nc = bass.Bass("TRN2", target_bir_lowering=False)
    xin = nc.dram_tensor("xin", [D, CORE_T + HALO], F32, kind="ExternalInput").ap()
    yout = nc.dram_tensor("yout", [D, CORE_T], F32, kind="ExternalOutput").ap()
    w_wq = nc.dram_tensor("w_wq", [D, D], F32, kind="ExternalInput").ap()
    w_wk = nc.dram_tensor("w_wk", [D, 256], F32, kind="ExternalInput").ap()
    w_wv = nc.dram_tensor("w_wv", [D, 256], F32, kind="ExternalInput").ap()
    w_wo = nc.dram_tensor("w_wo", [D, D], F32, kind="ExternalInput").ap()
    w_wr = nc.dram_tensor("w_wr", [D, D], F32, kind="ExternalInput").ap()
    w_wkf = nc.dram_tensor("w_wkf", [D, D], F32, kind="ExternalInput").ap()
    w_wvf = nc.dram_tensor("w_wvf", [D, D], F32, kind="ExternalInput").ap()
    whl = {}
    for nm, od in [("wr", D), ("wkf", D), ("wvf", D)]:
        whl[nm] = (nc.dram_tensor("hl_%s_h" % nm, [D, od], BF16).ap(),
                   nc.dram_tensor("hl_%s_l" % nm, [D, od], BF16).ap())
    c_ident = nc.dram_tensor("c_ident", [128, 128], F32, kind="ExternalInput").ap()
    c_logm = nc.dram_tensor("c_logm", [112, NH * 128], F32, kind="ExternalInput").ap()
    c_logmf = nc.dram_tensor("c_logmf", [112, NH * 128], F32, kind="ExternalInput").ap()
    c_logm64 = nc.dram_tensor("c_logm64", [64, NH * 80], F32, kind="ExternalInput").ap()
    c_selq = nc.dram_tensor("c_selq", [128, NCH * NH], F32, kind="ExternalInput").ap()
    c_selqT = nc.dram_tensor("c_selqT", [NH, NCH * 128], F32, kind="ExternalInput").ap()
    c_selk = nc.dram_tensor("c_selk", [128, 2 * NKV], F32, kind="ExternalInput").ap()
    c_selkT = nc.dram_tensor("c_selkT", [NKV, 2 * 128], F32, kind="ExternalInput").ap()
    c_qnw = nc.dram_tensor("c_qnw", [128, 1], F32, kind="ExternalInput").ap()
    c_knw = nc.dram_tensor("c_knw", [128, 1], F32, kind="ExternalInput").ap()
    c_mur = nc.dram_tensor("c_mur", [128, NCH], F32, kind="ExternalInput").ap()
    c_muk = nc.dram_tensor("c_muk", [128, NCH], F32, kind="ExternalInput").ap()
    c_scal = nc.dram_tensor("c_scal", [128, 8], F32, kind="ExternalInput").ap()

    sb = lambda name, shape: nc.alloc_sbuf_tensor(name, shape, F32).ap()
    pm = lambda name, shape: nc.alloc_psum_tensor(name, shape, F32).ap()
    xbuf = sb("xbuf", [128, NCH * TT])
    xnbuf = sb("xnbuf", [128, NCH * TT])
    scr = sb("scr", [128, NCH * TT])
    qbuf = sb("qbuf", [128, NCH * TILE])
    xmix = sb("xmix", [128, NCH * TILE])
    kbuf = sb("kbuf", [128, 2 * TT])
    vbuf = sb("vbuf", [128, 5 * 256])
    pbuf = sb("pbuf", [128, NH * 128])
    ptbuf = sb("ptbuf", [128, NH * 112])
    asub = sb("asub", [128, NCH * 112])
    lbuf = sb("lbuf", [128, 2 * NH])
    rbuf2 = sb("rbuf2", [128, 2 * 512])
    qrow = sb("qrow", [NH, 2 * TILE])
    krow = sb("krow", [NKV, 2 * TT])
    lrow = sb("lrow", [1, 4 * TT])
    wslot = sb("wslot", [128, 3 * 4096])
    ident = sb("ident", [128, 128])
    logm = sb("logm", [112, NH * 128])
    logmf = sb("logmf", [112, NH * 128])
    logm64 = sb("logm64", [64, NH * 80])
    selq = sb("selq", [128, NCH * NH])
    selqT = sb("selqT", [NH, NCH * 128])
    selk = sb("selk", [128, 2 * NKV])
    selkT = sb("selkT", [NKV, 2 * 128])
    qnw = sb("qnw", [128, 1])
    knw = sb("knw", [128, 1])
    mur = sb("mur", [128, NCH])
    muk = sb("muk", [128, NCH])
    scal = sb("scal", [128, 8])
    ones1 = sb("ones1", [1, 128])
    mur1m = sb("mur1m", [128, NCH])
    muk1m = sb("muk1m", [128, NCH])
    mu1m_ones = sb("mu1m_ones", [128, NCH])
    invD = sb("invD", [128, 1])
    psS = pm("psS", [128, 2048])
    psPV = pm("psPV", [128, 1024])
    psA = pm("psA", [128, 512])
    psB = pm("psB", [128, 512])
    with nc.Block() as block:
        s_c = _Sem(nc, "s_c")
        s_ws = [_Sem(nc, "s_ws0"), _Sem(nc, "s_ws1"), _Sem(nc, "s_ws2")]
        s_x = _Sem(nc, "s_x")
        s_y = _Sem(nc, "s_y")
        s_y1 = _Sem(nc, "s_y1")
        s_t = _Sem(nc, "s_t")
        s_v = _Sem(nc, "s_v")
        s_s = _Sem(nc, "s_s")
        s_g = _Sem(nc, "s_g")

        T = _EngRec("tensor")
        V = _EngRec("vector")
        S = _EngRec("scalar")
        G = _EngRec("gpsimd")
        SY = _EngRec("sync")

        C_SCALE = lambda p: scal[0:p, 0:1]
        C_EPSLN = lambda p: scal[0:p, 1:2]
        C_EPSRM = lambda p: scal[0:p, 2:3]
        C_NEG1 = lambda p: scal[0:p, 3:4]
        C_ONE = lambda p: scal[0:p, 4:5]
        C_ZERO = lambda p: scal[0:p, 5:6]

        for dst, src in [(scal, c_scal), (ident, c_ident), (logm, c_logm),
                         (logmf, c_logmf), (logm64, c_logm64), (selq, c_selq),
                         (selqT, c_selqT), (selk, c_selk), (selkT, c_selkT),
                         (qnw, c_qnw), (knw, c_knw), (mur, c_mur), (muk, c_muk)]:
            s_c.inc(SY.dma_start(out=dst[:, :], in_=src[:, :]), 16)
        CONST_W = s_c.n
        V.wait_ge(s_c.h, CONST_W)
        V.memset(ones1[:, :], 1.0)
        V.memset(invD[:, :], 1.0 / float(D))
        V.wait_ge(s_c.h, CONST_W)
        V.memset(mu1m_ones[:, :], 1.0)
        V.drain()
        V.scalar_tensor_tensor(mur1m[:, :], mur[:, :], C_NEG1(128), mu1m_ones[:, :],
                               op0=ALU.mult, op1=ALU.add)
        s_v.inc(V.scalar_tensor_tensor(muk1m[:, :], muk[:, :], C_NEG1(128),
                                       mu1m_ones[:, :], op0=ALU.mult, op1=ALU.add))
        INIT_V = s_v.n

        wq3 = w_wq.rearrange("(c p) o -> p c o", p=128)
        wo3 = w_wo.rearrange("(c p) o -> p c o", p=128)
        wr3 = w_wr.rearrange("(c p) o -> p c o", p=128)
        wkf3 = w_wkf.rearrange("(c p) o -> p c o", p=128)
        wvf3 = w_wvf.rearrange("(c p) o -> p c o", p=128)
        wk3 = w_wk.rearrange("(c p) o -> p c o", p=128)
        wv3 = w_wv.rearrange("(c p) o -> p c o", p=128)
        xin3 = xin.rearrange("(c p) t -> p c t", p=128)
        yout3 = yout.rearrange("(c p) t -> p c t", p=128)

        wslot_free = [None, None, None]

        def wslot_view(sl):
            return wslot[:, 4096 * sl: 4096 * (sl + 1)]

        def load_weight_half(src3, half, sl, kvpair=False):
            if wslot_free[sl] is not None:
                sem, cnt = wslot_free[sl]
                SY.wait_ge(sem, cnt)
            dst = wslot_view(sl)
            if kvpair:
                s_ws[sl].inc(SY.dma_start(
                    out=dst.rearrange("p (c o) -> p c o", o=512)[:, :, 0:256],
                    in_=wk3[:, :, :]), 16)
                s_ws[sl].inc(SY.dma_start(
                    out=dst.rearrange("p (c o) -> p c o", o=512)[:, :, 256:512],
                    in_=wv3[:, :, :]), 16)
            else:
                s_ws[sl].inc(SY.dma_start(
                    out=dst.rearrange("p (c o) -> p c o", o=512),
                    in_=src3[:, :, 512 * half: 512 * (half + 1)]), 16)
            return (sl, s_ws[sl].n)

        s_p = _Sem(nc, "s_p")
        s_po = _Sem(nc, "s_po")
        scrbf_all = scr.bitcast(BF16)
        xbv = xbuf
        prep_src = {"wr": wr3, "wkf": wkf3, "wvf": wvf3}
        whl3 = {nm: (h.rearrange("(c p) o -> p c o", p=128),
                     l.rearrange("(c p) o -> p c o", p=128))
                for nm, (h, l) in whl.items()}
        s_x0 = _Sem(nc, "s_x0")
        s_x0.inc(SY.dma_start(
            out=xbuf.rearrange("p (c t) -> p c t", t=TT),
            in_=xin3[:, :, 0:TT]), 16)

        wkv_info = load_weight_half(None, 0, 0, kvpair=True)
        wq_info = [load_weight_half(wq3, 0, 1), load_weight_half(wq3, 1, 2)]

        xmixbf_st = xmix.bitcast(BF16)
        prep_done = {}
        PREP = {"pv": 0, "pg": 0, "pi": 0}

        def emit_prep(nm, stage, first_gate):
            pi = PREP["pi"]
            for half in range(2):
                eng = V if pi % 2 == 0 else G
                s_eng = s_v if pi % 2 == 0 else s_g
                if first_gate is not None and half == 0:
                    SY.wait_ge(first_gate[0], first_gate[1])
                if pi > 0:
                    pe, s_pe = (G, s_g) if pi % 2 == 0 else (V, s_v)
                    if PREP["pv" if pe is V else "pg"]:
                        SY.wait_ge(s_pe.h, PREP["pv" if pe is V else "pg"])
                s_p.inc(SY.dma_start(
                    out=stage[:, 0:4096].rearrange("p (c o) -> p c o", o=512),
                    in_=prep_src[nm][:, :, 512 * half:512 * half + 512]), 16)
                eng.wait_ge(s_p.h, s_p.n)
                if pi > 0:
                    eng.wait_ge(s_po.h, 32 * pi)
                hi = xmixbf_st[:, 0:4096]
                lo = xmixbf_st[:, 4096:8192]
                eng.tensor_copy(hi, stage[:, 0:4096])
                eng.drain()
                s_eng.inc(eng.tensor_tensor(lo, stage[:, 0:4096], hi,
                                            op=ALU.subtract))
                PREP["pv" if eng is V else "pg"] = s_eng.n
                SY.wait_ge(s_eng.h, s_eng.n)
                s_po.inc(SY.dma_start(
                    out=whl3[nm][0][:, :, 512 * half:512 * half + 512],
                    in_=hi.rearrange("p (c o) -> p c o", o=512)), 16)
                s_po.inc(SY.dma_start(
                    out=whl3[nm][1][:, :, 512 * half:512 * half + 512],
                    in_=lo.rearrange("p (c o) -> p c o", o=512)), 16)
                prep_done[(nm, half)] = s_po.n
                pi += 1
            PREP["pi"] = pi

        def load_weight_half_hl(nm, half, sl):
            if wslot_free[sl] is not None:
                sem, cnt = wslot_free[sl]
                SY.wait_ge(sem, cnt)
            SY.wait_ge(s_po.h, prep_done[(nm, half)])
            dstb = wslot_view(sl).bitcast(BF16)
            s_ws[sl].inc(SY.dma_start(
                out=dstb[:, 0:4096].rearrange("p (c o) -> p c o", o=512),
                in_=whl3[nm][0][:, :, 512 * half: 512 * half + 512]), 16)
            s_ws[sl].inc(SY.dma_start(
                out=dstb[:, 4096:8192].rearrange("p (c o) -> p c o", o=512),
                in_=whl3[nm][1][:, :, 512 * half: 512 * half + 512]), 16)
            return (sl, s_ws[sl].n)



        bank_free = {"A": None, "B": None}

        def mm_gate(bank):
            if bank_free[bank] is not None:
                sem, cnt = bank_free[bank]
                T.wait_ge(sem, cnt)

        X_READY = None
        KMM_LAST = None
        LNS = {}
        VMM_LAST = None
        GADD_LAST = None
        PSS_GATE = None
        PSPV_GATE = None
        VEVAC_LAST = None
        RSIG_LAST = None
        KACT_G = None
        try:
          for tt in range(NT):
            if tt == 0:
                X_READY = (s_x0.h, s_x0.n)

            def emit_x2(first=(tt == 0)):
                V.wait_ge(X_READY[0], X_READY[1])
                if first:
                    V.wait_ge(s_c.h, CONST_W)
                if KMM_LAST is not None:
                    V.wait_ge(s_t.h, KMM_LAST)
                s_v.inc(V.tensor_tensor(scr[:, 0:NCH * TT], xbuf[:, 0:NCH * TT],
                                        xbuf[:, 0:NCH * TT], op=ALU.mult))
                LNS["XSQ_V"] = s_v.n

            def emit_ln_stats(first=(tt == 0)):
                if first:
                    emit_x2(first=True)
                xsq = LNS["XSQ_V"]
                T.wait_ge(X_READY[0], X_READY[1])
                T.wait_ge(s_v.h, INIT_V)
                if PSS_GATE is not None:
                    T.wait_ge(PSS_GATE[0], PSS_GATE[1])
                for ci in range(NCH):
                    T.matmul(psS[0:1, 0:512], invD[:, 0:1],
                             xbuf[:, TT * ci: TT * ci + 512],
                             start=(ci == 0), stop=(ci == NCH - 1))
                for ci in range(NCH):
                    T.matmul(psS[0:1, 512:TT], invD[:, 0:1],
                             xbuf[:, TT * ci + 512: TT * ci + TT],
                             start=(ci == 0), stop=(ci == NCH - 1))
                T.wait_ge(s_v.h, xsq)
                for ci in range(NCH):
                    T.matmul(psS[0:1, 1024:1536], invD[:, 0:1],
                             scr[:, TT * ci: TT * ci + 512],
                             start=(ci == 0), stop=(ci == NCH - 1))
                for ci in range(NCH):
                    inst = T.matmul(psS[0:1, 1536:1024 + TT], invD[:, 0:1],
                                    scr[:, TT * ci + 512: TT * ci + TT],
                                    start=(ci == 0), stop=(ci == NCH - 1))
                s_t.inc(inst)
                LNS["STATS_T"] = s_t.n

            if tt == 0:
                emit_ln_stats()
            STATS_T = LNS["STATS_T"]
            r0 = lrow[0:1, 0:TT]
            r1 = lrow[0:1, TT:2 * TT]
            r2 = lrow[0:1, 2 * TT:3 * TT]
            r3 = lrow[0:1, 3 * TT:4 * TT]
            V.wait_ge(s_t.h, STATS_T)
            V.tensor_copy(r0, psS[0:1, 0:TT])
            s_v.inc(V.tensor_copy(r1, psS[0:1, 1024:1024 + TT]))
            V.tensor_tensor(r2, r0, r0, op=ALU.mult)
            V.tensor_tensor(r1, r1, r2, op=ALU.subtract)
            V.tensor_scalar_add(r1, r1, C_EPSLN(1))
            s_v.inc(V.reciprocal(r2, r1))
            S.wait_ge(s_v.h, s_v.n)
            s_s.inc(S.activation(r3, r2, AF.Sqrt))
            V.wait_ge(s_s.h, s_s.n)
            s_v.inc(V.scalar_tensor_tensor(r2, r0, C_NEG1(1), r3,
                                           op0=ALU.mult, op1=ALU.mult))
            LNROWS_V = s_v.n
            T.wait_ge(s_v.h, LNROWS_V)
            T.matmul(psS[:, 0:512], ones1[0:1, :], r3[0:1, 0:512], start=True, stop=True)
            T.matmul(psS[:, 512:TT], ones1[0:1, :], r3[0:1, 512:TT], start=True, stop=True)
            T.matmul(psS[:, 1024:1536], ones1[0:1, :], r2[0:1, 0:512], start=True, stop=True)
            inst = T.matmul(psS[:, 1536:1024 + TT], ones1[0:1, :], r2[0:1, 512:TT],
                            start=True, stop=True)
            s_t.inc(inst)
            LNB_T = s_t.n
            V.wait_ge(s_t.h, LNB_T)
            xn_chunk_v = []
            for ci in range(NCH):
                if tt > 0 and ci == 0:
                    V.wait_ge(s_y.h, 32 * tt)
                if tt > 0 and ci == 4:
                    V.wait_ge(s_y1.h, 32 * tt)
                xs = xbuf[:, TT * ci: TT * ci + TT]
                xn_ = xnbuf[:, TT * ci: TT * ci + TT]
                V.tensor_tensor(xn_, xs, psS[:, 0:TT], op=ALU.mult)
                s_v.inc(V.tensor_tensor(xn_, xn_, psS[:, 1024:1024 + TT], op=ALU.add))
                xn_chunk_v.append(s_v.n)
            XN_V = s_v.n
            psS_free = (s_v.h, XN_V)
            if tt == 0:
                emit_prep("wr", scr, (s_t.h, STATS_T))
                emit_prep("wkf", scr, None)

            _chk(tt, "ln")
            (wkvidx, wkvcnt) = wkv_info
            T.wait_ge(s_ws[wkvidx].h, wkvcnt)
            wkv_v = wslot_view(wkvidx)
            nb = 0
            for ko in range(2):
                for (c0, c1) in [(0, 512), (512, TT)]:
                    bank = "A" if (nb % 2 == 0) else "B"
                    nb += 1
                    ps = psA if bank == "A" else psB
                    mm_gate(bank)
                    for ci in range(NCH):
                        T.wait_ge(s_v.h, xn_chunk_v[ci])
                        lhs = wkv_v[:, 512 * ci + 128 * ko: 512 * ci + 128 * ko + 128]
                        last = T.matmul(ps[0:128, 0:c1 - c0], lhs,
                                        xnbuf[:, TT * ci + c0: TT * ci + c1],
                                        start=(ci == 0), stop=(ci == NCH - 1))
                    s_t.inc(last)
                    S.wait_ge(s_t.h, s_t.n)
                    s_s.inc(S.copy(kbuf[:, TT * ko + c0: TT * ko + c1],
                                   ps[0:128, 0:c1 - c0]))
                    bank_free[bank] = (s_s.h, s_s.n)
            VPROJ_V = [None]

            def emit_vproj():
                for g, (goff, glen) in enumerate(VGROUPS):
                    bank = "A" if (g % 2 == 0) else "B"
                    ps = psA if bank == "A" else psB
                    mm_gate(bank)
                    for ci in range(NCH):
                        lhs = xnbuf[:, TT * ci + goff: TT * ci + goff + glen]
                        last = T.matmul(ps[0:glen, 0:256], lhs,
                                        wkv_v[:, 512 * ci + 256: 512 * ci + 512],
                                        start=(ci == 0), stop=(ci == NCH - 1))
                    s_t.inc(last)
                    V.wait_ge(s_t.h, s_t.n)
                    s_v.inc(V.tensor_copy(vbuf[0:glen, 256 * g: 256 * g + 256],
                                          ps[0:glen, 0:256]))
                    bank_free[bank] = (s_v.h, s_v.n)
                wslot_free[wkvidx] = (s_t.h, s_t.n)
                VPROJ_V[0] = s_v.n

            _chk(tt, "kv")
            if VEVAC_LAST is not None:
                S.wait_ge(s_v.h, VEVAC_LAST)
            for half in range(2):
                (wqi, wqc) = wq_info[half]
                T.wait_ge(s_ws[wqi].h, wqc)
                wq_v = wslot_view(wqi)
                for co in range(4 * half, 4 * half + 4):
                    bank = "A" if (co % 2 == 0) else "B"
                    ps = psA if bank == "A" else psB
                    mm_gate(bank)
                    for ci in range(NCH):
                        lhs = wq_v[:, 512 * ci + 128 * (co % 4):
                                   512 * ci + 128 * (co % 4) + 128]
                        last = T.matmul(ps[0:128, 0:512], lhs,
                                        xnbuf[:, TT * ci + HALO: TT * ci + TT],
                                        start=(ci == 0), stop=(ci == NCH - 1))
                    s_t.inc(last)
                    S.wait_ge(s_t.h, s_t.n)
                    s_s.inc(S.copy(qbuf[:, 512 * co: 512 * co + 512],
                                   ps[0:128, 0:512]))
                    bank_free[bank] = (s_s.h, s_s.n)
                wslot_free[wqi] = (s_t.h, s_t.n)
            wr_info = [load_weight_half_hl("wr", 0, 1), None]
            Q_PROJ_S = s_s.n

            xncur = xnbuf.rearrange("p (c t) -> p c t", t=TT)[:, :, HALO:TT]
            xnprev = xnbuf.rearrange("p (c t) -> p c t", t=TT)[:, :, HALO - 1:TT - 1]
            xmixbf = xmix.bitcast(BF16)
            scrbf = scr.bitcast(BF16)
            tmp2 = [pbuf[:, 1024:1536], pbuf[:, 1536:2048]]
            gsp = {}

            def emit_mix(idx, mu_, mu1m_, dstbf, extra_t=None):
                sl = tmp2[idx % 2]
                ci = idx % NCH
                if idx >= 2:
                    V.wait_ge(s_g.h, gsp[idx - 2])
                elif KACT_G is not None:
                    V.wait_ge(s_g.h, KACT_G)
                V.tensor_scalar_mul(sl, xnbuf[:, TT * ci + HALO: TT * ci + TT],
                                    mu_[:, ci:ci + 1])
                V.drain()
                s_v.inc(V.scalar_tensor_tensor(
                    sl, xnbuf[:, TT * ci + HALO - 1: TT * ci + TT - 1],
                    mu1m_[:, ci:ci + 1], sl, op0=ALU.mult, op1=ALU.add))
                G.wait_ge(s_v.h, s_v.n)
                if extra_t is not None and ci == 0:
                    G.wait_ge(s_t.h, extra_t)
                if tt == 0 and idx == 0:
                    G.wait_ge(s_po.h, prep_done[("wkf", 1)])
                if tt == 0 and idx == NCH:
                    G.wait_ge(s_po.h, prep_done[("wkf", 1)])
                G.tensor_copy(dstbf[:, 512 * ci: 512 * ci + 512], sl)
                G.drain()
                s_g.inc(G.tensor_tensor(
                    dstbf[:, 4096 + 512 * ci: 4096 + 512 * ci + 512], sl,
                    dstbf[:, 512 * ci: 512 * ci + 512], op=ALU.subtract))
                gsp[idx] = s_g.n

            for ci in range(NCH):
                emit_mix(ci, mur, mur1m, xmixbf, extra_t=VMM_LAST)
            XR_V = s_g.n
            for ci in range(NCH):
                emit_mix(NCH + ci, muk, muk1m, scrbf, extra_t=STATS_T)
            XK_V = s_g.n
            XK_STT_V = s_v.n

            _chk(tt, "qproj")
            def emit_r_filler(co, bank):
                (wi, wc) = wr_info[0]
                T.wait_ge(s_ws[wi].h, wc)
                T.wait_ge(s_g.h, XR_V)
                wbf = wslot_view(wi).bitcast(BF16)
                ps = psA if bank == "A" else psB
                mm_gate(bank)
                np_ = 3 * NCH
                k = 0
                for ci in range(NCH):
                    whc = wbf[:, 512 * ci + 128 * (co % 4):
                              512 * ci + 128 * (co % 4) + 128]
                    wlc = wbf[:, 4096 + 512 * ci + 128 * (co % 4):
                              4096 + 512 * ci + 128 * (co % 4) + 128]
                    rh = xmixbf[:, 512 * ci: 512 * ci + 512]
                    rl = xmixbf[:, 4096 + 512 * ci: 4096 + 512 * ci + 512]
                    for lhs, rhs_ in ((whc, rh), (wlc, rh), (whc, rl)):
                        last = T.matmul(ps[0:128, 0:512], lhs, rhs_,
                                        start=(k == 0), stop=(k == np_ - 1))
                        k += 1
                s_t.inc(last)
                S.wait_ge(s_t.h, s_t.n)
                if VEVAC_LAST is not None:
                    S.wait_ge(s_v.h, VEVAC_LAST)
                s_s.inc(S.activation(rbuf2[:, 512 * co: 512 * co + 512],
                                     ps[0:128, 0:512], AF.Sigmoid))
                bank_free[bank] = (s_s.h, s_s.n)

            V.wait_ge(s_s.h, Q_PROJ_S)
            if GADD_LAST is not None:
                V.wait_ge(s_g.h, GADD_LAST)
            mm_gate("A")
            qsq_mm_t = {}
            for ci in range(NCH):
                sl = pbuf[:, 512 * (ci % 2): 512 * (ci % 2) + 512]
                if ci >= 2:
                    V.wait_ge(s_t.h, qsq_mm_t[ci - 2])
                s_v.inc(V.tensor_tensor(sl, qbuf[:, 512 * ci: 512 * ci + 512],
                                        qbuf[:, 512 * ci: 512 * ci + 512],
                                        op=ALU.mult))
                T.wait_ge(s_v.h, s_v.n)
                last = T.matmul(psA[0:NH, 0:512], selq[:, NH * ci: NH * ci + NH],
                                sl, start=(ci == 0), stop=(ci == NCH - 1))
                s_t.inc(last)
                qsq_mm_t[ci] = s_t.n
            emit_r_filler(0, "B")
            V.wait_ge(s_t.h, qsq_mm_t[NCH - 1])
            s_v.inc(V.tensor_scalar_add(qrow[0:NH, 0:512], psA[0:NH, 0:512],
                                        C_EPSRM(NH)))
            bank_free["A"] = (s_v.h, s_v.n)
            s_v.inc(V.reciprocal(qrow[0:NH, 512:1024], qrow[0:NH, 0:512]))
            S.wait_ge(s_v.h, s_v.n)
            s_s.inc(S.activation(qrow[0:NH, 0:512], qrow[0:NH, 512:1024], AF.Sqrt))
            QR_S = s_s.n
            ksl = [ptbuf[:, 0:TT], ptbuf[:, TT:2 * TT]]
            V.wait_ge(s_t.h, qsq_mm_t[NCH - 1])
            for ci in range(2):
                s_v.inc(V.tensor_tensor(ksl[ci], kbuf[:, TT * ci: TT * ci + TT],
                                        kbuf[:, TT * ci: TT * ci + TT],
                                        op=ALU.mult))
            KSQ_V = s_v.n
            T.wait_ge(s_v.h, KSQ_V)
            if PSPV_GATE is not None:
                T.wait_ge(PSPV_GATE[0], PSPV_GATE[1])
            for (c0, c1, pcol) in [(0, 512, 0), (512, TT, 512)]:
                for ci in range(2):
                    last = T.matmul(psPV[0:NKV, pcol:pcol + (c1 - c0)],
                                    selk[:, NKV * ci: NKV * ci + NKV],
                                    ksl[ci][:, c0:c1],
                                    start=(ci == 0), stop=(ci == 1))
            s_t.inc(last)
            V.wait_ge(s_t.h, s_t.n)
            V.tensor_scalar_add(krow[0:NKV, 0:512], psPV[0:NKV, 0:512], C_EPSRM(NKV))
            s_v.inc(V.tensor_scalar_add(krow[0:NKV, 512:TT], psPV[0:NKV, 512:TT],
                                        C_EPSRM(NKV)))
            psPV_free = (s_v.h, s_v.n)
            s_v.inc(V.reciprocal(krow[0:NKV, TT:2 * TT], krow[0:NKV, 0:TT]))
            S.wait_ge(s_v.h, s_v.n)
            s_s.inc(S.activation(krow[0:NKV, 0:TT], krow[0:NKV, TT:2 * TT], AF.Sqrt))
            KR_S = s_s.n
            _chk(tt, "rms_rows")
            T.wait_ge(s_s.h, QR_S)
            for ci in range(NCH):
                if ci % 2 == 0:
                    mm_gate("B")
                    ps = psB[0:128, 0:512]
                else:
                    T.wait_ge(psS_free[0], psS_free[1])
                    ps = psS[:, 0:512]
                last = T.matmul(ps, selqT[0:NH, 128 * ci: 128 * ci + 128],
                                qrow[0:NH, 0:512], start=True, stop=True)
                s_t.inc(last)
                V.wait_ge(s_t.h, s_t.n)
                s_v.inc(V.scalar_tensor_tensor(
                    qbuf[:, 512 * ci: 512 * ci + 512],
                    qbuf[:, 512 * ci: 512 * ci + 512], qnw[:, 0:1], ps,
                    op0=ALU.mult, op1=ALU.mult))
                if ci % 2 == 0:
                    bank_free["B"] = (s_v.h, s_v.n)
                else:
                    psS_free = (s_v.h, s_v.n)
            Q_APPLIED_V = s_v.n
            T.wait_ge(s_s.h, KR_S)
            for ci in range(2):
                for (c0, c1) in [(0, 512), (512, TT)]:
                    mm_gate("A")
                    last = T.matmul(psA[0:128, 0:c1 - c0],
                                    selkT[0:NKV, 128 * ci: 128 * ci + 128],
                                    krow[0:NKV, c0:c1], start=True, stop=True)
                    s_t.inc(last)
                    V.wait_ge(s_t.h, s_t.n)
                    s_v.inc(V.scalar_tensor_tensor(
                        kbuf[:, TT * ci + c0: TT * ci + c1],
                        kbuf[:, TT * ci + c0: TT * ci + c1], knw[:, 0:1],
                        psA[0:128, 0:c1 - c0], op0=ALU.mult, op1=ALU.mult))
                    bank_free["A"] = (s_v.h, s_v.n)
            K_APPLIED_V = s_v.n
            emit_r_filler(1, "A")

            _chk(tt, "rms")
            ptbuf_evac_s = None
            pv_evac_v = None
            prev = None

            def emit_qk(ss, qoff, qlen, kw):
                nonlocal_vals = {}
                msk = logmf if (tt == 0 and ss == 0) else (logm if qlen == 112 else logm64)
                T.wait_ge(s_v.h, max(Q_APPLIED_V, K_APPLIED_V))
                if ptbuf_evac_s is not None:
                    T.wait_ge(s_s.h, ptbuf_evac_s)
                else:
                    T.wait_ge(psS_free[0], psS_free[1])
                for s in range(NH):
                    h = 2 * (s % 8) + s // 8
                    pq = 64 * (h % 2)
                    ciq = h // 2
                    cik = (h % 4) // 2
                    lhs = qbuf[pq:pq + 64, 512 * ciq + qoff: 512 * ciq + qoff + qlen]
                    rhs = kbuf[pq:pq + 64, TT * cik + qoff: TT * cik + qoff + kw]
                    last = T.matmul(psS[0:qlen, 128 * s: 128 * s + kw], lhs, rhs,
                                    start=True, stop=True)
                s_t.inc(last)
                return msk, s_t.n

            def emit_softmax(msk, qk_t, qlen, kw, first_sub=False):
                V.wait_ge(s_t.h, qk_t)
                if first_sub:
                    V.wait_ge(s_g.h, XK_V)
                sin = psS[0:qlen, 0:NH * 128].rearrange(
                    "q (h k) -> q h k", h=NH)[:, :, 0:kw]
                min_ = msk[0:qlen, 0:NH * kw].rearrange("q (h k) -> q h k", h=NH)
                pout = pbuf[0:qlen, 0:NH * kw].rearrange("q (h k) -> q h k", h=NH)
                s_v.inc(V.scalar_tensor_tensor(pout, sin, C_SCALE(qlen), min_,
                                               op0=ALU.mult, op1=ALU.add))
                S.wait_ge(s_v.h, s_v.n)
                s_s.inc(S.activation(pbuf[0:qlen, 0:NH * kw],
                                     pbuf[0:qlen, 0:NH * kw], AF.Exp))
                V.wait_ge(s_s.h, s_s.n)
                s_v.inc(V.tensor_reduce(lbuf[0:qlen, 0:NH], pout, axis=AX.X,
                                        op=ALU.add))
                s_v.inc(V.reciprocal(lbuf[0:qlen, NH:2 * NH], lbuf[0:qlen, 0:NH]))
                rb = lbuf[0:qlen, NH:2 * NH].unsqueeze(-1).broadcast_to(
                    [qlen, NH, kw])
                s_v.inc(V.tensor_tensor(pout, pout, rb, op=ALU.mult))
                return s_v.n

            def emit_transposes(pnorm_v, qlen, kw):
                nonlocal ptbuf_evac_s
                T.wait_ge(s_v.h, pnorm_v)
                per_bank = 4 if qlen == 112 else 8
                nbanks = NH // per_bank
                bank_t = []
                for b in range(nbanks):
                    for j in range(per_bank):
                        s = b * per_bank + j
                        tin = pbuf[0:qlen, kw * s: kw * s + kw]
                        col = 512 * b + qlen * j
                        last = T.transpose(psS[0:kw, col: col + qlen], tin,
                                           ident[0:qlen, 0:qlen])
                    s_t.inc(last)
                    bank_t.append(s_t.n)
                for b in range(nbanks):
                    S.wait_ge(s_t.h, bank_t[b])
                    s_s.inc(S.copy(
                        ptbuf[0:kw, per_bank * qlen * b: per_bank * qlen * (b + 1)],
                        psS[0:kw, 512 * b: 512 * b + per_bank * qlen]))
                ptbuf_evac_s = s_s.n

            def emit_pv(ss, qoff, qlen, kw, ptevac, vproj_v):
                nonlocal pv_evac_v
                T.wait_ge(s_s.h, ptevac)
                T.wait_ge(s_v.h, vproj_v)
                if pv_evac_v is not None:
                    T.wait_ge(s_s.h, pv_evac_v)
                else:
                    T.wait_ge(psPV_free[0], psPV_free[1])
                for half in range(2):
                    for h in range(8 * half, 8 * half + 8):
                        s = (h % 2) * 8 + h // 2
                        kv = h % 4
                        lhs = vbuf[0:kw, 256 * ss + 64 * kv: 256 * ss + 64 * kv + 64]
                        rhs = ptbuf[0:kw, qlen * s: qlen * s + qlen]
                        pp = 64 * (h % 2)
                        col = 128 * (h // 2)
                        last = T.matmul(psPV[pp:pp + 64, col: col + qlen], lhs, rhs,
                                        start=True, stop=True)
                    s_t.inc(last)
                    S.wait_ge(s_t.h, s_t.n)
                    pvin = psPV[:, 512 * half: 512 * half + 512].rearrange(
                        "p (c t) -> p c t", t=128)[:, :, 0:qlen]
                    aout = asub[:, 448 * half: 448 * half + 448].rearrange(
                        "p (c t) -> p c t", t=112)[:, :, 0:qlen]
                    s_s.inc(S.copy(aout, pvin))
                pv_evac_v = s_s.n

            def emit_oproj(ss, qoff, qlen, kw, *_):
                T.wait_ge(s_ws[wo_info[0][0]].h, wo_info[0][1])
                T.wait_ge(s_ws[wo_info[1][0]].h, wo_info[1][1])
                T.wait_ge(s_s.h, pv_evac_v)
                for co in range(NCH):
                    wo_v = wslot_view(wo_info[0][0] if co < 4 else wo_info[1][0])
                    bank = "A" if (co % 2 == 0) else "B"
                    ps = psA if bank == "A" else psB
                    mm_gate(bank)
                    for ci in range(NCH):
                        lhs = wo_v[:, 512 * ci + 128 * (co % 4):
                                   512 * ci + 128 * (co % 4) + 128]
                        last = T.matmul(ps[0:128, 0:qlen], lhs,
                                        asub[:, 112 * ci: 112 * ci + qlen],
                                        start=(ci == 0), stop=(ci == NCH - 1))
                    s_t.inc(last)
                    V.wait_ge(s_t.h, s_t.n)
                    xsl = xbuf[:, TT * co + HALO + qoff: TT * co + HALO + qoff + qlen]
                    ysl = xnbuf[:, TT * co + HALO + qoff: TT * co + HALO + qoff + qlen]
                    s_v.inc(V.scalar_tensor_tensor(ysl, ps[0:128, 0:qlen], C_ONE(128),
                                                   xsl, op0=ALU.mult, op1=ALU.add))
                    bank_free[bank] = (s_v.h, s_v.n)

            for ss, (qoff, qlen) in enumerate(SUBTILES):
                kw = qlen + 16
                msk, qk_t = emit_qk(ss, qoff, qlen, kw)
                if prev is not None:
                    emit_pv(*prev)
                pnorm_v = emit_softmax(msk, qk_t, qlen, kw, first_sub=(ss == 0))
                if ss == 0:
                    emit_vproj()
                    wo_info = [load_weight_half(wo3, 0, 0), load_weight_half(wo3, 1, 2)]
                if prev is not None:
                    emit_oproj(*prev)
                emit_transposes(pnorm_v, qlen, kw)
                prev = (ss, qoff, qlen, kw, ptbuf_evac_s, VPROJ_V[0])
            emit_pv(*prev)
            emit_oproj(*prev)
            YADDS_V = s_v.n
            PSS_GATE = (s_s.h, ptbuf_evac_s)
            PSPV_GATE = (s_s.h, pv_evac_v)
            wslot_free[wo_info[0][0]] = (s_t.h, s_t.n)
            wslot_free[wo_info[1][0]] = (s_t.h, s_t.n)
            wkf_info = [load_weight_half_hl("wkf", 0, 0), load_weight_half_hl("wkf", 1, 2)]
            if tt == 0:
                SY.wait_ge(s_v.h, YADDS_V)
                xbufbf = xbuf.bitcast(BF16)
                pi_q = 0
                prev_split = None
                prev_out = None
                for halfq in range(2):
                    for q in range(4):
                        eng = V if pi_q % 2 == 0 else G
                        s_eng = s_v if pi_q % 2 == 0 else s_g
                        if prev_split is not None:
                            SY.wait_ge(prev_split[0], prev_split[1])
                        s_p.inc(SY.dma_start(
                            out=xbuf[:, 0:1024].rearrange("p (c o) -> p c o", o=512),
                            in_=prep_src["wvf"][:, 2 * q:2 * q + 2,
                                                512 * halfq:512 * halfq + 512]), 16)
                        eng.wait_ge(s_p.h, s_p.n)
                        if prev_out is not None:
                            eng.wait_ge(s_po.h, prev_out)
                        hiq = xbufbf[:, 4096:5120]
                        loq = xbufbf[:, 5120:6144]
                        eng.tensor_copy(hiq, xbuf[:, 0:1024])
                        eng.drain()
                        s_eng.inc(eng.tensor_tensor(loq, xbuf[:, 0:1024], hiq,
                                                    op=ALU.subtract))
                        PREP["pv" if eng is V else "pg"] = s_eng.n
                        SY.wait_ge(s_eng.h, s_eng.n)
                        s_po.inc(SY.dma_start(
                            out=whl3["wvf"][0][:, 2 * q:2 * q + 2,
                                               512 * halfq:512 * halfq + 512],
                            in_=hiq.rearrange("p (c o) -> p c o", o=512)), 16)
                        s_po.inc(SY.dma_start(
                            out=whl3["wvf"][1][:, 2 * q:2 * q + 2,
                                               512 * halfq:512 * halfq + 512],
                            in_=loq.rearrange("p (c o) -> p c o", o=512)), 16)
                        prev_split = (s_eng.h, s_eng.n)
                        prev_out = s_po.n
                        pi_q += 1
                    prep_done[("wvf", halfq)] = s_po.n
            if tt + 1 < NT:
                G.wait_ge(s_v.h, YADDS_V)
                if tt == 0:
                    G.wait_ge(s_po.h, prep_done[("wvf", 1)])
                s_x.inc(G.dma_start(
                    out=xbuf.rearrange("p (c t) -> p c t", t=TT),
                    in_=xin3[:, :, TILE * (tt + 1): TILE * (tt + 1) + TT]), 32)
                X_READY = (s_x.h, s_x.n)

            _chk(tt, "attn")

            def ffn_phase(w_infos, rhs_buf, evac, stride=512, co0=0, co1=NCH,
                          mid_cb=None, mid_at=4):
                for co in range(co0, co1):
                    if co == mid_at and mid_cb is not None:
                        mid_cb()
                    (wi, wc) = w_infos[0] if co < 4 else w_infos[1]
                    T.wait_ge(s_ws[wi].h, wc)
                    wv_ = wslot_view(wi)
                    bank = "A" if (co % 2 == 0) else "B"
                    ps = psA if bank == "A" else psB
                    mm_gate(bank)
                    for ci in range(NCH):
                        lhs = wv_[:, 512 * ci + 128 * (co % 4):
                                  512 * ci + 128 * (co % 4) + 128]
                        last = T.matmul(ps[0:128, 0:512], lhs,
                                        rhs_buf[:, stride * ci: stride * ci + 512],
                                        start=(ci == 0), stop=(ci == NCH - 1))
                    s_t.inc(last)
                    evac(co, ps, s_t.n, bank)

            def ffn_phase_hl(w_infos, rhsbf, evac, co0=0, co1=NCH,
                             mid_cb=None, mid_at=4):
                for co in range(co0, co1):
                    if co == mid_at and mid_cb is not None:
                        mid_cb()
                    (wi, wc) = w_infos[0] if co < 4 else w_infos[1]
                    T.wait_ge(s_ws[wi].h, wc)
                    wbf = wslot_view(wi).bitcast(BF16)
                    bank = "A" if (co % 2 == 0) else "B"
                    ps = psA if bank == "A" else psB
                    mm_gate(bank)
                    np_ = 3 * NCH
                    k = 0
                    for ci in range(NCH):
                        whc = wbf[:, 512 * ci + 128 * (co % 4):
                                  512 * ci + 128 * (co % 4) + 128]
                        wlc = wbf[:, 4096 + 512 * ci + 128 * (co % 4):
                                  4096 + 512 * ci + 128 * (co % 4) + 128]
                        rh = rhsbf[:, 512 * ci: 512 * ci + 512]
                        rl = rhsbf[:, 4096 + 512 * ci: 4096 + 512 * ci + 512]
                        for lhs, rhs_ in ((whc, rh), (wlc, rh), (whc, rl)):
                            last = T.matmul(ps[0:128, 0:512], lhs, rhs_,
                                            start=(k == 0), stop=(k == np_ - 1))
                            k += 1
                    s_t.inc(last)
                    evac(co, ps, s_t.n, bank)

            _chk(tt, "ffnmix")
            T.wait_ge(s_g.h, XR_V)
            S.wait_ge(s_v.h, XK_STT_V)

            def r_evac(co, ps, tcnt, bank):
                S.wait_ge(s_t.h, tcnt)
                s_s.inc(S.activation(qbuf[:, 512 * co: 512 * co + 512],
                                     ps[0:128, 0:512], AF.Sigmoid))
                bank_free[bank] = (s_s.h, s_s.n)
            ffn_phase_hl([wr_info[0], wr_info[0]], xmixbf, r_evac, co0=2, co1=4)
            wslot_free[wr_info[0][0]] = (s_t.h, s_t.n)
            wr_info[1] = load_weight_half_hl("wr", 1, 1)
            ffn_phase_hl([None, wr_info[1]], xmixbf, r_evac, co0=4)
            RSIG_LAST = s_s.n
            RMM_LAST = s_t.n
            wslot_free[wr_info[1][0]] = (s_t.h, s_t.n)
            wvf_info = [load_weight_half_hl("wvf", 0, 1), None]
            _chk(tt, "ffnr")
            T.wait_ge(s_g.h, XK_V)
            ksp = {}

            def k_evac(co, ps, tcnt, bank):
                sl = tmp2[co % 2]
                V.wait_ge(s_t.h, tcnt)
                if co < 2:
                    V.wait_ge(s_g.h, XK_V)
                else:
                    V.wait_ge(s_g.h, ksp[co - 2])
                s_v.inc(V.tensor_scalar_max(sl, ps[0:128, 0:512], C_ZERO(128)))
                bank_free[bank] = (s_v.h, s_v.n)
                S.wait_ge(s_v.h, s_v.n)
                s_s.inc(S.activation(sl, sl, AF.Square))
                G.wait_ge(s_s.h, s_s.n)
                if co == 0:
                    G.wait_ge(s_t.h, RMM_LAST)
                G.tensor_copy(xmixbf[:, 512 * co: 512 * co + 512], sl)
                G.drain()
                s_g.inc(G.tensor_tensor(
                    xmixbf[:, 4096 + 512 * co: 4096 + 512 * co + 512], sl,
                    xmixbf[:, 512 * co: 512 * co + 512], op=ALU.subtract))
                ksp[co] = s_g.n
            ffn_phase_hl(wkf_info, scrbf, k_evac)
            KMM_LAST = s_t.n
            if tt + 1 < NT:
                emit_x2(first=False)
            wslot_free[wkf_info[0][0]] = (s_t.h, s_t.n)
            wslot_free[wkf_info[1][0]] = (s_t.h, s_t.n)
            wvf_info[1] = load_weight_half_hl("wvf", 1, 0)
            _chk(tt, "ffnk")
            KACT_G = s_g.n
            T.wait_ge(s_g.h, KACT_G)

            gadd_n = {}

            def v_evac(co, ps, tcnt, bank):
                V.wait_ge(s_t.h, tcnt)
                tsl = pbuf[:, 512 * (co % 2): 512 * (co % 2) + 512]
                if co >= 2:
                    V.wait_ge(s_g.h, gadd_n[co - 2])
                rsrc = (rbuf2[:, 512 * co: 512 * co + 512] if co < 2
                        else qbuf[:, 512 * co: 512 * co + 512])
                s_v.inc(V.tensor_tensor(tsl, ps[0:128, 0:512], rsrc,
                                        op=ALU.mult))
                bank_free[bank] = (s_v.h, s_v.n)
                G.wait_ge(s_v.h, s_v.n)
                ysl = xnbuf[:, TT * co + HALO: TT * co + TT]
                s_g.inc(G.tensor_tensor(ysl, ysl, tsl, op=ALU.add))
                gadd_n[co] = s_g.n
                if co == 3 or co == 7:
                    hb = 0 if co == 3 else 1
                    ysem = s_y if hb == 0 else s_y1
                    ysem.inc(G.dma_start(
                        out=yout3[:, 4 * hb:4 * hb + 4, TILE * tt: TILE * tt + TILE],
                        in_=xnbuf.rearrange("p (c t) -> p c t", t=TT)
                        [:, 4 * hb:4 * hb + 4, HALO:TT]), 32)

            def stats_cb():
                if tt + 1 < NT:
                    emit_ln_stats(first=False)
            ffn_phase_hl(wvf_info, xmixbf, v_evac, mid_cb=stats_cb, mid_at=6)
            VMM_LAST = s_t.n
            VEVAC_LAST = s_v.n
            GADD_LAST = s_g.n
            wslot_free[wvf_info[0][0]] = (s_t.h, s_t.n)
            wslot_free[wvf_info[1][0]] = (s_t.h, s_t.n)
            if tt + 1 < NT:
                wkv_info = load_weight_half(None, 0, 0, kvpair=True)
                wq_info = [load_weight_half(wq3, 0, 1), load_weight_half(wq3, 1, 2)]

        except _StopBuild:
            SY.wait_ge(s_t.h, s_t.n)
            SY.wait_ge(s_v.h, s_v.n)
            SY.wait_ge(s_s.h, s_s.n)
            SY.wait_ge(s_g.h, s_g.n)
            done = s_y.n // 32
            for t2 in range(done, NT):
                s_y.inc(G.dma_start(
                    out=yout3[:, 0:4, TILE * t2: TILE * t2 + TILE],
                    in_=xbuf.rearrange("p (c t) -> p c t", t=TT)[:, 0:4, HALO:TT]), 32)
                s_y1.inc(G.dma_start(
                    out=yout3[:, 4:8, TILE * t2: TILE * t2 + TILE],
                    in_=xbuf.rearrange("p (c t) -> p c t", t=TT)[:, 4:8, HALO:TT]), 32)

        G.wait_ge(s_y.h, 32 * NT)
        G.wait_ge(s_y1.h, 32 * NT)

        block.sync(SY._flush)
        block.tensor(T._flush)
        block.vector(V._flush)
        block.scalar(S._flush)
        block.gpsimd(G._flush)

    return nc


_NC_CACHE = [None]


def kernel(L0_attn_x, L0_attn_wq, L0_attn_wk, L0_attn_wv, L0_attn_wo,
           L0_attn_q_norm_w, L0_attn_k_norm_w, L0_ffn_W_r, L0_ffn_W_k,
           L0_ffn_W_v, L0_ffn_mu_r, L0_ffn_mu_k):
    x = np.asarray(L0_attn_x, dtype=np.float32)
    f32c = lambda a: np.ascontiguousarray(np.asarray(a, dtype=np.float32))
    wq_t = f32c(np.asarray(L0_attn_wq).T)
    wk_t = f32c(np.asarray(L0_attn_wk).T)
    wv_t = f32c(np.asarray(L0_attn_wv).T)
    wo_t = f32c(np.asarray(L0_attn_wo).T)
    wr_t = f32c(np.asarray(L0_ffn_W_r).T)
    wkf_t = f32c(np.asarray(L0_ffn_W_k).T)
    wvf_t = f32c(np.asarray(L0_ffn_W_v).T)
    qnw = f32c(np.tile(np.asarray(L0_attn_q_norm_w), 2)[:, None])
    knw = f32c(np.tile(np.asarray(L0_attn_k_norm_w), 2)[:, None])
    mur = f32c(np.asarray(L0_ffn_mu_r).reshape(NCH, 128).T)
    muk = f32c(np.asarray(L0_ffn_mu_k).reshape(NCH, 128).T)
    scal = f32c(np.tile(np.array([0.125, 1e-5, 0.01, -1.0, 1.0, 0.0, 0.0, 0.0],
                                 dtype=np.float32), (128, 1)))
    cn = _consts()

    in_maps = []
    for c in range(8):
        b, h = c // 2, c % 2
        xt = x[b].T
        if h == 0:
            xcore = np.concatenate(
                [np.zeros((D, HALO), dtype=np.float32), xt[:, :CORE_T]], axis=1)
            lmf = cn["logmf"]
        else:
            xcore = xt[:, CORE_T - HALO: 2 * CORE_T]
            lmf = cn["logm"]
        in_maps.append({
            "xin": np.ascontiguousarray(xcore),
            "w_wq": wq_t, "w_wk": wk_t, "w_wv": wv_t, "w_wo": wo_t,
            "w_wr": wr_t, "w_wkf": wkf_t, "w_wvf": wvf_t,
            "c_ident": cn["ident"], "c_logm": cn["logm"], "c_logmf": lmf,
            "c_logm64": cn["logm64"], "c_selq": cn["selq"], "c_selqT": cn["selqT"],
            "c_selk": cn["selk"], "c_selkT": cn["selkT"],
            "c_qnw": qnw, "c_knw": knw, "c_mur": mur, "c_muk": muk, "c_scal": scal,
        })

    if _NC_CACHE[0] is None:
        _NC_CACHE[0] = build_program()
    nc = _NC_CACHE[0]
    res = run_bass_kernel_spmd(nc, in_maps, core_ids=list(range(8)))

    y = np.empty((B, L, D), dtype=np.float32)
    for c in range(8):
        b, h = c // 2, c % 2
        y[b, h * CORE_T:(h + 1) * CORE_T, :] = res.results[c]["yout"].T
    return y


# version 18
# speedup vs baseline: 1.0401x; 1.0401x over previous
import math
import numpy as np
import concourse.bass as bass
import concourse.mybir as mybir
from concourse.bass_utils import run_bass_kernel_spmd

F32 = mybir.dt.float32
BF16 = mybir.dt.bfloat16
AF = mybir.ActivationFunctionType
ALU = mybir.AluOpType
AX = mybir.AxisListType

D = 1024
NCH = 8
NH = 16
NKV = 4
HD = 64
B, L = 4, 4096
CORE_T = 2048
HALO = 16
TILE = 512
NT = CORE_T // TILE
TT = TILE + HALO
SUBTILES = [(0, 112), (112, 112), (224, 112), (336, 112), (448, 64)]
VGROUPS = [(0, 128), (112, 128), (224, 128), (336, 128), (448, 80)]
NEG = -1e30
SHIFT = 8.0


def _slopes():
    return np.array([2.0 ** (-0.5 * (h + 1)) for h in range(NH)], dtype=np.float64)


def _mask_tile(qlen, first):
    kw = qlen + 16
    sl = _slopes()
    m = np.full((qlen, NH, kw), NEG, dtype=np.float64)
    for i in range(qlen):
        for j in range(i, min(kw, i + 17)):
            if first and j < 16:
                continue
            m[i, :, j] = sl * (j - i - 16) - SHIFT
    m = m[:, [2 * (s % 8) + s // 8 for s in range(NH)], :]
    return m.astype(np.float32).reshape(qlen, NH * kw)


def _consts():
    c = {}
    c["ident"] = np.eye(128, dtype=np.float32)
    c["logm"] = _mask_tile(112, False)
    c["logmf"] = _mask_tile(112, True)
    c["logm64"] = _mask_tile(64, False)
    selq = np.zeros((128, NCH * NH), dtype=np.float32)
    selqT = np.zeros((NH, NCH * 128), dtype=np.float32)
    for ci in range(NCH):
        for p in range(128):
            h = 2 * ci + (1 if p >= 64 else 0)
            selq[p, 16 * ci + h] = 1.0 / 64.0
            selqT[h, 128 * ci + p] = 1.0
    c["selq"], c["selqT"] = selq, selqT
    selk = np.zeros((128, 2 * NKV), dtype=np.float32)
    selkT = np.zeros((NKV, 2 * 128), dtype=np.float32)
    for ci in range(2):
        for p in range(128):
            kv = 2 * ci + (1 if p >= 64 else 0)
            selk[p, 4 * ci + kv] = 1.0 / 64.0
            selkT[kv, 128 * ci + p] = 1.0
    c["selk"], c["selkT"] = selk, selkT
    return c


_DBG_STOP = [None]


class _StopBuild(Exception):
    pass


def _chk(tt, tag):
    st = _DBG_STOP[0]
    if st is None:
        return
    stile, stag = st
    if tt > stile or (tt == stile and tag == stag):
        raise _StopBuild()


class _Sem:
    def __init__(self, nc, name):
        self.h = nc.semaphore(name).__enter__()
        self.n = 0

    def inc(self, inst, k=1):
        inst.then_inc(self.h, k)
        self.n += k


class _DefInst:
    __slots__ = ("incs",)

    def __init__(self):
        self.incs = []

    def then_inc(self, sem, k):
        self.incs.append((sem, k))
        return self


def _ap_span(ap):
    try:
        t = ap.tensor.name
        pairs = [list(p) for p in ap.ap]
        bp = ap.base_partition
        if callable(bp):
            bp = bp()
        bp = bp or 0
        np_ = pairs[0][1] if pairs else 128
        lo = ap.offset
        hi = ap.offset
        for st, cnt in pairs[1:]:
            if st > 0:
                hi += st * (cnt - 1)
            elif st < 0:
                lo += st * (cnt - 1)
        return (t, bp, bp + np_, lo, hi + 1)
    except Exception:
        return None


def _overlap(s1, s2):
    return (s1[0] == s2[0] and s1[1] < s2[2] and s2[1] < s1[2]
            and s1[3] < s2[4] and s2[3] < s1[4])


class _EngRec:

    _HAZARD = ("vector", "scalar", "gpsimd")

    def __init__(self, name):
        self._name = name
        self._ops = []
        self._wr = []
        self._rd = []

    def __getattr__(self, attr):
        if attr.startswith("_"):
            raise AttributeError(attr)
        rec = self

        def call(*a, **kw):
            if rec._name in _EngRec._HAZARD and attr not in ("wait_ge", "drain"):
                wsp, rsp = [], []
                args = list(a)
                for i, arg in enumerate(args):
                    sp = _ap_span(arg) if hasattr(arg, "tensor") else None
                    if sp is None:
                        continue
                    (wsp if i == 0 else rsp).append(sp)
                for k2, arg in kw.items():
                    sp = _ap_span(arg) if hasattr(arg, "tensor") else None
                    if sp is None:
                        continue
                    (wsp if k2 == "accum_out" else rsp).append(sp)
                conflict = any(_overlap(s, w) for s in wsp + rsp for w in rec._wr)                     or any(_overlap(s, r) for s in wsp for r in rec._rd)
                if conflict:
                    rec._ops.append(("drain", (), {}, _DefInst()))
                    rec._wr, rec._rd = [], []
                rec._wr.extend(wsp)
                rec._rd.extend(rsp)
            di = _DefInst()
            rec._ops.append((attr, a, kw, di))
            return di
        return call

    def _flush(self, eng):
        for attr, a, kw, di in self._ops:
            inst = getattr(eng, attr)(*a, **kw)
            for sem, k in di.incs:
                inst.then_inc(sem, k)


def build_program():
    nc = bass.Bass("TRN2", target_bir_lowering=False)
    xin = nc.dram_tensor("xin", [D, CORE_T + HALO], F32, kind="ExternalInput").ap()
    yout = nc.dram_tensor("yout", [D, CORE_T], F32, kind="ExternalOutput").ap()
    w_wq = nc.dram_tensor("w_wq", [D, D], F32, kind="ExternalInput").ap()
    w_wk = nc.dram_tensor("w_wk", [D, 256], F32, kind="ExternalInput").ap()
    w_wv = nc.dram_tensor("w_wv", [D, 256], F32, kind="ExternalInput").ap()
    w_wo = nc.dram_tensor("w_wo", [D, D], F32, kind="ExternalInput").ap()
    w_wr = nc.dram_tensor("w_wr", [D, D], F32, kind="ExternalInput").ap()
    w_wkf = nc.dram_tensor("w_wkf", [D, D], F32, kind="ExternalInput").ap()
    w_wvf = nc.dram_tensor("w_wvf", [D, D], F32, kind="ExternalInput").ap()
    whl = {}
    for nm, od in [("wr", D), ("wkf", D), ("wvf", D)]:
        whl[nm] = (nc.dram_tensor("hl_%s_h" % nm, [D, od], BF16).ap(),
                   nc.dram_tensor("hl_%s_l" % nm, [D, od], BF16).ap())
    c_ident = nc.dram_tensor("c_ident", [128, 128], F32, kind="ExternalInput").ap()
    c_logm = nc.dram_tensor("c_logm", [112, NH * 128], F32, kind="ExternalInput").ap()
    c_logmf = nc.dram_tensor("c_logmf", [112, NH * 128], F32, kind="ExternalInput").ap()
    c_logm64 = nc.dram_tensor("c_logm64", [64, NH * 80], F32, kind="ExternalInput").ap()
    c_selq = nc.dram_tensor("c_selq", [128, NCH * NH], F32, kind="ExternalInput").ap()
    c_selqT = nc.dram_tensor("c_selqT", [NH, NCH * 128], F32, kind="ExternalInput").ap()
    c_selk = nc.dram_tensor("c_selk", [128, 2 * NKV], F32, kind="ExternalInput").ap()
    c_selkT = nc.dram_tensor("c_selkT", [NKV, 2 * 128], F32, kind="ExternalInput").ap()
    c_qnw = nc.dram_tensor("c_qnw", [128, 1], F32, kind="ExternalInput").ap()
    c_knw = nc.dram_tensor("c_knw", [128, 1], F32, kind="ExternalInput").ap()
    c_mur = nc.dram_tensor("c_mur", [128, NCH], F32, kind="ExternalInput").ap()
    c_muk = nc.dram_tensor("c_muk", [128, NCH], F32, kind="ExternalInput").ap()
    c_scal = nc.dram_tensor("c_scal", [128, 8], F32, kind="ExternalInput").ap()

    sb = lambda name, shape: nc.alloc_sbuf_tensor(name, shape, F32).ap()
    pm = lambda name, shape: nc.alloc_psum_tensor(name, shape, F32).ap()
    xbuf = sb("xbuf", [128, NCH * TT])
    xnbuf = sb("xnbuf", [128, NCH * TT])
    scr = sb("scr", [128, NCH * TT])
    qbuf = sb("qbuf", [128, NCH * TILE])
    xmix = sb("xmix", [128, NCH * TILE])
    kbuf = sb("kbuf", [128, 2 * TT])
    vbuf = sb("vbuf", [128, 5 * 256])
    pbuf = sb("pbuf", [128, NH * 128])
    ptbuf = sb("ptbuf", [128, NH * 112])
    asub = sb("asub", [128, NCH * 112])
    lbuf = sb("lbuf", [128, 2 * NH])
    rbuf2 = sb("rbuf2", [128, 2 * 512])
    qrow = sb("qrow", [NH, 2 * TILE])
    krow = sb("krow", [NKV, 2 * TT])
    lrow = sb("lrow", [1, 4 * TT])
    wslot = sb("wslot", [128, 3 * 4096])
    ident = sb("ident", [128, 128])
    logm = sb("logm", [112, NH * 128])
    logmf = sb("logmf", [112, NH * 128])
    logm64 = sb("logm64", [64, NH * 80])
    selq = sb("selq", [128, NCH * NH])
    selqT = sb("selqT", [NH, NCH * 128])
    selk = sb("selk", [128, 2 * NKV])
    selkT = sb("selkT", [NKV, 2 * 128])
    qnw = sb("qnw", [128, 1])
    knw = sb("knw", [128, 1])
    mur = sb("mur", [128, NCH])
    muk = sb("muk", [128, NCH])
    scal = sb("scal", [128, 8])
    ones1 = sb("ones1", [1, 128])
    mur1m = sb("mur1m", [128, NCH])
    muk1m = sb("muk1m", [128, NCH])
    mu1m_ones = sb("mu1m_ones", [128, NCH])
    invD = sb("invD", [128, 1])
    psS = pm("psS", [128, 2048])
    psPV = pm("psPV", [128, 1024])
    psA = pm("psA", [128, 512])
    psB = pm("psB", [128, 512])
    with nc.Block() as block:
        s_c = _Sem(nc, "s_c")
        s_ws = [_Sem(nc, "s_ws0"), _Sem(nc, "s_ws1"), _Sem(nc, "s_ws2")]
        s_x = _Sem(nc, "s_x")
        s_y = _Sem(nc, "s_y")
        s_y1 = _Sem(nc, "s_y1")
        s_t = _Sem(nc, "s_t")
        s_v = _Sem(nc, "s_v")
        s_s = _Sem(nc, "s_s")
        s_g = _Sem(nc, "s_g")

        T = _EngRec("tensor")
        V = _EngRec("vector")
        S = _EngRec("scalar")
        G = _EngRec("gpsimd")
        SY = _EngRec("sync")

        C_SCALE = lambda p: scal[0:p, 0:1]
        C_EPSLN = lambda p: scal[0:p, 1:2]
        C_EPSRM = lambda p: scal[0:p, 2:3]
        C_NEG1 = lambda p: scal[0:p, 3:4]
        C_ONE = lambda p: scal[0:p, 4:5]
        C_ZERO = lambda p: scal[0:p, 5:6]

        for dst, src in [(scal, c_scal), (ident, c_ident), (logm, c_logm),
                         (logmf, c_logmf), (logm64, c_logm64), (selq, c_selq),
                         (selqT, c_selqT), (selk, c_selk), (selkT, c_selkT),
                         (qnw, c_qnw), (knw, c_knw), (mur, c_mur), (muk, c_muk)]:
            s_c.inc(SY.dma_start(out=dst[:, :], in_=src[:, :]), 16)
        CONST_W = s_c.n
        V.wait_ge(s_c.h, CONST_W)
        V.memset(ones1[:, :], 1.0)
        V.memset(invD[:, :], 1.0 / float(D))
        V.wait_ge(s_c.h, CONST_W)
        V.memset(mu1m_ones[:, :], 1.0)
        V.drain()
        V.scalar_tensor_tensor(mur1m[:, :], mur[:, :], C_NEG1(128), mu1m_ones[:, :],
                               op0=ALU.mult, op1=ALU.add)
        s_v.inc(V.scalar_tensor_tensor(muk1m[:, :], muk[:, :], C_NEG1(128),
                                       mu1m_ones[:, :], op0=ALU.mult, op1=ALU.add))
        INIT_V = s_v.n

        wq3 = w_wq.rearrange("(c p) o -> p c o", p=128)
        wo3 = w_wo.rearrange("(c p) o -> p c o", p=128)
        wr3 = w_wr.rearrange("(c p) o -> p c o", p=128)
        wkf3 = w_wkf.rearrange("(c p) o -> p c o", p=128)
        wvf3 = w_wvf.rearrange("(c p) o -> p c o", p=128)
        wk3 = w_wk.rearrange("(c p) o -> p c o", p=128)
        wv3 = w_wv.rearrange("(c p) o -> p c o", p=128)
        xin3 = xin.rearrange("(c p) t -> p c t", p=128)
        yout3 = yout.rearrange("(c p) t -> p c t", p=128)

        wslot_free = [None, None, None]

        def wslot_view(sl):
            return wslot[:, 4096 * sl: 4096 * (sl + 1)]

        def load_weight_half(src3, half, sl, kvpair=False):
            if wslot_free[sl] is not None:
                sem, cnt = wslot_free[sl]
                SY.wait_ge(sem, cnt)
            dst = wslot_view(sl)
            if kvpair:
                s_ws[sl].inc(SY.dma_start(
                    out=dst.rearrange("p (c o) -> p c o", o=512)[:, :, 0:256],
                    in_=wk3[:, :, :]), 16)
                s_ws[sl].inc(SY.dma_start(
                    out=dst.rearrange("p (c o) -> p c o", o=512)[:, :, 256:512],
                    in_=wv3[:, :, :]), 16)
            else:
                s_ws[sl].inc(SY.dma_start(
                    out=dst.rearrange("p (c o) -> p c o", o=512),
                    in_=src3[:, :, 512 * half: 512 * (half + 1)]), 16)
            return (sl, s_ws[sl].n)

        s_p = _Sem(nc, "s_p")
        s_po = _Sem(nc, "s_po")
        scrbf_all = scr.bitcast(BF16)
        xbv = xbuf
        prep_src = {"wr": wr3, "wkf": wkf3, "wvf": wvf3}
        whl3 = {nm: (h.rearrange("(c p) o -> p c o", p=128),
                     l.rearrange("(c p) o -> p c o", p=128))
                for nm, (h, l) in whl.items()}
        s_x0 = _Sem(nc, "s_x0")
        s_x0.inc(SY.dma_start(
            out=xbuf.rearrange("p (c t) -> p c t", t=TT),
            in_=xin3[:, :, 0:TT]), 16)

        wkv_info = load_weight_half(None, 0, 0, kvpair=True)
        wq_info = [load_weight_half(wq3, 0, 1), load_weight_half(wq3, 1, 2)]

        xmixbf_st = xmix.bitcast(BF16)
        prep_done = {}
        PREP = {"pv": 0, "pg": 0, "pi": 0}

        def emit_prep(nm, stage, first_gate):
            pi = PREP["pi"]
            for half in range(2):
                if first_gate is not None and half == 0:
                    SY.wait_ge(first_gate[0], first_gate[1])
                if pi > 0:
                    SY.wait_ge(s_v.h, PREP["pv"])
                s_p.inc(SY.dma_start(
                    out=stage[:, 0:4096].rearrange("p (c o) -> p c o", o=512),
                    in_=prep_src[nm][:, :, 512 * half:512 * half + 512]), 16)
                hi = xmixbf_st[:, 0:4096]
                lo = xmixbf_st[:, 4096:8192]
                S.wait_ge(s_p.h, s_p.n)
                if pi > 0:
                    S.wait_ge(s_po.h, 32 * pi)
                s_s.inc(S.copy(hi, stage[:, 0:4096]))
                V.wait_ge(s_s.h, s_s.n)
                s_v.inc(V.tensor_tensor(lo, stage[:, 0:4096], hi,
                                        op=ALU.subtract))
                PREP["pv"] = s_v.n
                SY.wait_ge(s_v.h, s_v.n)
                s_po.inc(SY.dma_start(
                    out=whl3[nm][0][:, :, 512 * half:512 * half + 512],
                    in_=hi.rearrange("p (c o) -> p c o", o=512)), 16)
                s_po.inc(SY.dma_start(
                    out=whl3[nm][1][:, :, 512 * half:512 * half + 512],
                    in_=lo.rearrange("p (c o) -> p c o", o=512)), 16)
                prep_done[(nm, half)] = s_po.n
                pi += 1
            PREP["pi"] = pi

        def load_weight_half_hl(nm, half, sl):
            if wslot_free[sl] is not None:
                sem, cnt = wslot_free[sl]
                SY.wait_ge(sem, cnt)
            SY.wait_ge(s_po.h, prep_done[(nm, half)])
            dstb = wslot_view(sl).bitcast(BF16)
            s_ws[sl].inc(SY.dma_start(
                out=dstb[:, 0:4096].rearrange("p (c o) -> p c o", o=512),
                in_=whl3[nm][0][:, :, 512 * half: 512 * half + 512]), 16)
            s_ws[sl].inc(SY.dma_start(
                out=dstb[:, 4096:8192].rearrange("p (c o) -> p c o", o=512),
                in_=whl3[nm][1][:, :, 512 * half: 512 * half + 512]), 16)
            return (sl, s_ws[sl].n)



        bank_free = {"A": None, "B": None}

        def mm_gate(bank):
            if bank_free[bank] is not None:
                sem, cnt = bank_free[bank]
                T.wait_ge(sem, cnt)

        X_READY = None
        KMM_LAST = None
        LNS = {}
        VMM_LAST = None
        GADD_LAST = None
        PSS_GATE = None
        PSPV_GATE = None
        VEVAC_LAST = None
        RSIG_LAST = None
        KACT_G = None
        try:
          for tt in range(NT):
            if tt == 0:
                X_READY = (s_x0.h, s_x0.n)

            def emit_x2(first=(tt == 0)):
                V.wait_ge(X_READY[0], X_READY[1])
                if first:
                    V.wait_ge(s_c.h, CONST_W)
                if KMM_LAST is not None:
                    V.wait_ge(s_t.h, KMM_LAST)
                s_v.inc(V.tensor_tensor(scr[:, 0:NCH * TT], xbuf[:, 0:NCH * TT],
                                        xbuf[:, 0:NCH * TT], op=ALU.mult))
                LNS["XSQ_V"] = s_v.n

            def emit_ln_stats(first=(tt == 0)):
                if first:
                    emit_x2(first=True)
                xsq = LNS["XSQ_V"]
                T.wait_ge(X_READY[0], X_READY[1])
                T.wait_ge(s_v.h, INIT_V)
                if PSS_GATE is not None:
                    T.wait_ge(PSS_GATE[0], PSS_GATE[1])
                for ci in range(NCH):
                    T.matmul(psS[0:1, 0:512], invD[:, 0:1],
                             xbuf[:, TT * ci: TT * ci + 512],
                             start=(ci == 0), stop=(ci == NCH - 1))
                for ci in range(NCH):
                    T.matmul(psS[0:1, 512:TT], invD[:, 0:1],
                             xbuf[:, TT * ci + 512: TT * ci + TT],
                             start=(ci == 0), stop=(ci == NCH - 1))
                T.wait_ge(s_v.h, xsq)
                for ci in range(NCH):
                    T.matmul(psS[0:1, 1024:1536], invD[:, 0:1],
                             scr[:, TT * ci: TT * ci + 512],
                             start=(ci == 0), stop=(ci == NCH - 1))
                for ci in range(NCH):
                    inst = T.matmul(psS[0:1, 1536:1024 + TT], invD[:, 0:1],
                                    scr[:, TT * ci + 512: TT * ci + TT],
                                    start=(ci == 0), stop=(ci == NCH - 1))
                s_t.inc(inst)
                LNS["STATS_T"] = s_t.n

            if tt == 0:
                emit_ln_stats()
            STATS_T = LNS["STATS_T"]
            r0 = lrow[0:1, 0:TT]
            r1 = lrow[0:1, TT:2 * TT]
            r2 = lrow[0:1, 2 * TT:3 * TT]
            r3 = lrow[0:1, 3 * TT:4 * TT]
            V.wait_ge(s_t.h, STATS_T)
            V.tensor_copy(r0, psS[0:1, 0:TT])
            s_v.inc(V.tensor_copy(r1, psS[0:1, 1024:1024 + TT]))
            V.tensor_tensor(r2, r0, r0, op=ALU.mult)
            V.tensor_tensor(r1, r1, r2, op=ALU.subtract)
            V.tensor_scalar_add(r1, r1, C_EPSLN(1))
            s_v.inc(V.reciprocal(r2, r1))
            S.wait_ge(s_v.h, s_v.n)
            s_s.inc(S.activation(r3, r2, AF.Sqrt))
            V.wait_ge(s_s.h, s_s.n)
            s_v.inc(V.scalar_tensor_tensor(r2, r0, C_NEG1(1), r3,
                                           op0=ALU.mult, op1=ALU.mult))
            LNROWS_V = s_v.n
            T.wait_ge(s_v.h, LNROWS_V)
            T.matmul(psS[:, 0:512], ones1[0:1, :], r3[0:1, 0:512], start=True, stop=True)
            T.matmul(psS[:, 512:TT], ones1[0:1, :], r3[0:1, 512:TT], start=True, stop=True)
            T.matmul(psS[:, 1024:1536], ones1[0:1, :], r2[0:1, 0:512], start=True, stop=True)
            inst = T.matmul(psS[:, 1536:1024 + TT], ones1[0:1, :], r2[0:1, 512:TT],
                            start=True, stop=True)
            s_t.inc(inst)
            LNB_T = s_t.n
            V.wait_ge(s_t.h, LNB_T)
            xn_chunk_v = []
            for ci in range(NCH):
                if tt > 0 and ci == 0:
                    V.wait_ge(s_y.h, 32 * tt)
                if tt > 0 and ci == 4:
                    V.wait_ge(s_y1.h, 32 * tt)
                xs = xbuf[:, TT * ci: TT * ci + TT]
                xn_ = xnbuf[:, TT * ci: TT * ci + TT]
                V.tensor_tensor(xn_, xs, psS[:, 0:TT], op=ALU.mult)
                s_v.inc(V.tensor_tensor(xn_, xn_, psS[:, 1024:1024 + TT], op=ALU.add))
                xn_chunk_v.append(s_v.n)
            XN_V = s_v.n
            psS_free = (s_v.h, XN_V)
            if tt == 0:
                emit_prep("wr", scr, (s_t.h, STATS_T))
                emit_prep("wkf", scr, None)
                emit_prep("wvf", scr, None)

            _chk(tt, "ln")
            (wkvidx, wkvcnt) = wkv_info
            T.wait_ge(s_ws[wkvidx].h, wkvcnt)
            wkv_v = wslot_view(wkvidx)
            nb = 0
            for ko in range(2):
                for (c0, c1) in [(0, 512), (512, TT)]:
                    bank = "A" if (nb % 2 == 0) else "B"
                    nb += 1
                    ps = psA if bank == "A" else psB
                    mm_gate(bank)
                    for ci in range(NCH):
                        T.wait_ge(s_v.h, xn_chunk_v[ci])
                        lhs = wkv_v[:, 512 * ci + 128 * ko: 512 * ci + 128 * ko + 128]
                        last = T.matmul(ps[0:128, 0:c1 - c0], lhs,
                                        xnbuf[:, TT * ci + c0: TT * ci + c1],
                                        start=(ci == 0), stop=(ci == NCH - 1))
                    s_t.inc(last)
                    S.wait_ge(s_t.h, s_t.n)
                    s_s.inc(S.copy(kbuf[:, TT * ko + c0: TT * ko + c1],
                                   ps[0:128, 0:c1 - c0]))
                    bank_free[bank] = (s_s.h, s_s.n)
            VPROJ_V = [None]

            def emit_vproj():
                for g, (goff, glen) in enumerate(VGROUPS):
                    bank = "A" if (g % 2 == 0) else "B"
                    ps = psA if bank == "A" else psB
                    mm_gate(bank)
                    for ci in range(NCH):
                        lhs = xnbuf[:, TT * ci + goff: TT * ci + goff + glen]
                        last = T.matmul(ps[0:glen, 0:256], lhs,
                                        wkv_v[:, 512 * ci + 256: 512 * ci + 512],
                                        start=(ci == 0), stop=(ci == NCH - 1))
                    s_t.inc(last)
                    V.wait_ge(s_t.h, s_t.n)
                    s_v.inc(V.tensor_copy(vbuf[0:glen, 256 * g: 256 * g + 256],
                                          ps[0:glen, 0:256]))
                    bank_free[bank] = (s_v.h, s_v.n)
                wslot_free[wkvidx] = (s_t.h, s_t.n)
                VPROJ_V[0] = s_v.n

            _chk(tt, "kv")
            if VEVAC_LAST is not None:
                S.wait_ge(s_v.h, VEVAC_LAST)
            for half in range(2):
                (wqi, wqc) = wq_info[half]
                T.wait_ge(s_ws[wqi].h, wqc)
                wq_v = wslot_view(wqi)
                for co in range(4 * half, 4 * half + 4):
                    bank = "A" if (co % 2 == 0) else "B"
                    ps = psA if bank == "A" else psB
                    mm_gate(bank)
                    for ci in range(NCH):
                        lhs = wq_v[:, 512 * ci + 128 * (co % 4):
                                   512 * ci + 128 * (co % 4) + 128]
                        last = T.matmul(ps[0:128, 0:512], lhs,
                                        xnbuf[:, TT * ci + HALO: TT * ci + TT],
                                        start=(ci == 0), stop=(ci == NCH - 1))
                    s_t.inc(last)
                    S.wait_ge(s_t.h, s_t.n)
                    s_s.inc(S.copy(qbuf[:, 512 * co: 512 * co + 512],
                                   ps[0:128, 0:512]))
                    bank_free[bank] = (s_s.h, s_s.n)
                wslot_free[wqi] = (s_t.h, s_t.n)
            wr_info = [load_weight_half_hl("wr", 0, 1), None]
            Q_PROJ_S = s_s.n

            xncur = xnbuf.rearrange("p (c t) -> p c t", t=TT)[:, :, HALO:TT]
            xnprev = xnbuf.rearrange("p (c t) -> p c t", t=TT)[:, :, HALO - 1:TT - 1]
            xmixbf = xmix.bitcast(BF16)
            scrbf = scr.bitcast(BF16)
            tmp2 = [pbuf[:, 1024:1536], pbuf[:, 1536:2048]]
            gsp = {}

            def emit_mix(idx, mu_, mu1m_, dstbf, extra_t=None):
                sl = tmp2[idx % 2]
                ci = idx % NCH
                V.tensor_scalar_mul(sl, xnbuf[:, TT * ci + HALO: TT * ci + TT],
                                    mu_[:, ci:ci + 1])
                V.drain()
                V.scalar_tensor_tensor(
                    sl, xnbuf[:, TT * ci + HALO - 1: TT * ci + TT - 1],
                    mu1m_[:, ci:ci + 1], sl, op0=ALU.mult, op1=ALU.add)
                if extra_t is not None and ci == 0:
                    V.wait_ge(s_t.h, extra_t)
                if tt == 0 and idx == 0:
                    V.wait_ge(s_po.h, prep_done[("wvf", 1)])
                if tt == 0 and idx == NCH:
                    V.wait_ge(s_po.h, prep_done[("wvf", 1)])
                V.tensor_copy(dstbf[:, 512 * ci: 512 * ci + 512], sl)
                V.drain()
                s_v.inc(V.tensor_tensor(
                    dstbf[:, 4096 + 512 * ci: 4096 + 512 * ci + 512], sl,
                    dstbf[:, 512 * ci: 512 * ci + 512], op=ALU.subtract))

            for ci in range(NCH):
                emit_mix(ci, mur, mur1m, xmixbf, extra_t=VMM_LAST)
            XR_V = s_v.n
            for ci in range(NCH):
                emit_mix(NCH + ci, muk, muk1m, scrbf, extra_t=STATS_T)
            XK_V = s_v.n
            XK_STT_V = s_v.n

            _chk(tt, "qproj")
            def emit_r_filler(co, bank):
                (wi, wc) = wr_info[0]
                T.wait_ge(s_ws[wi].h, wc)
                T.wait_ge(s_v.h, XR_V)
                wbf = wslot_view(wi).bitcast(BF16)
                ps = psA if bank == "A" else psB
                mm_gate(bank)
                np_ = 3 * NCH
                k = 0
                for ci in range(NCH):
                    whc = wbf[:, 512 * ci + 128 * (co % 4):
                              512 * ci + 128 * (co % 4) + 128]
                    wlc = wbf[:, 4096 + 512 * ci + 128 * (co % 4):
                              4096 + 512 * ci + 128 * (co % 4) + 128]
                    rh = xmixbf[:, 512 * ci: 512 * ci + 512]
                    rl = xmixbf[:, 4096 + 512 * ci: 4096 + 512 * ci + 512]
                    for lhs, rhs_ in ((whc, rh), (wlc, rh), (whc, rl)):
                        last = T.matmul(ps[0:128, 0:512], lhs, rhs_,
                                        start=(k == 0), stop=(k == np_ - 1))
                        k += 1
                s_t.inc(last)
                S.wait_ge(s_t.h, s_t.n)
                if VEVAC_LAST is not None:
                    S.wait_ge(s_v.h, VEVAC_LAST)
                s_s.inc(S.activation(rbuf2[:, 512 * co: 512 * co + 512],
                                     ps[0:128, 0:512], AF.Sigmoid))
                bank_free[bank] = (s_s.h, s_s.n)

            V.wait_ge(s_s.h, Q_PROJ_S)
            if GADD_LAST is not None:
                V.wait_ge(s_g.h, GADD_LAST)
            mm_gate("A")
            qsq_mm_t = {}
            for ci in range(NCH):
                sl = pbuf[:, 512 * (ci % 2): 512 * (ci % 2) + 512]
                if ci >= 2:
                    V.wait_ge(s_t.h, qsq_mm_t[ci - 2])
                s_v.inc(V.tensor_tensor(sl, qbuf[:, 512 * ci: 512 * ci + 512],
                                        qbuf[:, 512 * ci: 512 * ci + 512],
                                        op=ALU.mult))
                T.wait_ge(s_v.h, s_v.n)
                last = T.matmul(psA[0:NH, 0:512], selq[:, NH * ci: NH * ci + NH],
                                sl, start=(ci == 0), stop=(ci == NCH - 1))
                s_t.inc(last)
                qsq_mm_t[ci] = s_t.n
            emit_r_filler(0, "B")
            V.wait_ge(s_t.h, qsq_mm_t[NCH - 1])
            s_v.inc(V.tensor_scalar_add(qrow[0:NH, 0:512], psA[0:NH, 0:512],
                                        C_EPSRM(NH)))
            bank_free["A"] = (s_v.h, s_v.n)
            s_v.inc(V.reciprocal(qrow[0:NH, 512:1024], qrow[0:NH, 0:512]))
            S.wait_ge(s_v.h, s_v.n)
            s_s.inc(S.activation(qrow[0:NH, 0:512], qrow[0:NH, 512:1024], AF.Sqrt))
            QR_S = s_s.n
            ksl = [ptbuf[:, 0:TT], ptbuf[:, TT:2 * TT]]
            V.wait_ge(s_t.h, qsq_mm_t[NCH - 1])
            for ci in range(2):
                s_v.inc(V.tensor_tensor(ksl[ci], kbuf[:, TT * ci: TT * ci + TT],
                                        kbuf[:, TT * ci: TT * ci + TT],
                                        op=ALU.mult))
            KSQ_V = s_v.n
            T.wait_ge(s_v.h, KSQ_V)
            if PSPV_GATE is not None:
                T.wait_ge(PSPV_GATE[0], PSPV_GATE[1])
            for (c0, c1, pcol) in [(0, 512, 0), (512, TT, 512)]:
                for ci in range(2):
                    last = T.matmul(psPV[0:NKV, pcol:pcol + (c1 - c0)],
                                    selk[:, NKV * ci: NKV * ci + NKV],
                                    ksl[ci][:, c0:c1],
                                    start=(ci == 0), stop=(ci == 1))
            s_t.inc(last)
            V.wait_ge(s_t.h, s_t.n)
            V.tensor_scalar_add(krow[0:NKV, 0:512], psPV[0:NKV, 0:512], C_EPSRM(NKV))
            s_v.inc(V.tensor_scalar_add(krow[0:NKV, 512:TT], psPV[0:NKV, 512:TT],
                                        C_EPSRM(NKV)))
            psPV_free = (s_v.h, s_v.n)
            s_v.inc(V.reciprocal(krow[0:NKV, TT:2 * TT], krow[0:NKV, 0:TT]))
            S.wait_ge(s_v.h, s_v.n)
            s_s.inc(S.activation(krow[0:NKV, 0:TT], krow[0:NKV, TT:2 * TT], AF.Sqrt))
            KR_S = s_s.n
            _chk(tt, "rms_rows")
            T.wait_ge(s_s.h, QR_S)
            for ci in range(NCH):
                if ci % 2 == 0:
                    mm_gate("B")
                    ps = psB[0:128, 0:512]
                else:
                    T.wait_ge(psS_free[0], psS_free[1])
                    ps = psS[:, 0:512]
                last = T.matmul(ps, selqT[0:NH, 128 * ci: 128 * ci + 128],
                                qrow[0:NH, 0:512], start=True, stop=True)
                s_t.inc(last)
                V.wait_ge(s_t.h, s_t.n)
                s_v.inc(V.scalar_tensor_tensor(
                    qbuf[:, 512 * ci: 512 * ci + 512],
                    qbuf[:, 512 * ci: 512 * ci + 512], qnw[:, 0:1], ps,
                    op0=ALU.mult, op1=ALU.mult))
                if ci % 2 == 0:
                    bank_free["B"] = (s_v.h, s_v.n)
                else:
                    psS_free = (s_v.h, s_v.n)
            Q_APPLIED_V = s_v.n
            T.wait_ge(s_s.h, KR_S)
            for ci in range(2):
                for (c0, c1) in [(0, 512), (512, TT)]:
                    mm_gate("A")
                    last = T.matmul(psA[0:128, 0:c1 - c0],
                                    selkT[0:NKV, 128 * ci: 128 * ci + 128],
                                    krow[0:NKV, c0:c1], start=True, stop=True)
                    s_t.inc(last)
                    V.wait_ge(s_t.h, s_t.n)
                    s_v.inc(V.scalar_tensor_tensor(
                        kbuf[:, TT * ci + c0: TT * ci + c1],
                        kbuf[:, TT * ci + c0: TT * ci + c1], knw[:, 0:1],
                        psA[0:128, 0:c1 - c0], op0=ALU.mult, op1=ALU.mult))
                    bank_free["A"] = (s_v.h, s_v.n)
            K_APPLIED_V = s_v.n
            emit_r_filler(1, "A")

            _chk(tt, "rms")
            ptbuf_evac_s = None
            pv_evac_v = None
            prev = None

            def emit_qk(ss, qoff, qlen, kw):
                nonlocal_vals = {}
                msk = logmf if (tt == 0 and ss == 0) else (logm if qlen == 112 else logm64)
                T.wait_ge(s_v.h, max(Q_APPLIED_V, K_APPLIED_V))
                if ptbuf_evac_s is not None:
                    T.wait_ge(s_s.h, ptbuf_evac_s)
                else:
                    T.wait_ge(psS_free[0], psS_free[1])
                for s in range(NH):
                    h = 2 * (s % 8) + s // 8
                    pq = 64 * (h % 2)
                    ciq = h // 2
                    cik = (h % 4) // 2
                    lhs = qbuf[pq:pq + 64, 512 * ciq + qoff: 512 * ciq + qoff + qlen]
                    rhs = kbuf[pq:pq + 64, TT * cik + qoff: TT * cik + qoff + kw]
                    last = T.matmul(psS[0:qlen, 128 * s: 128 * s + kw], lhs, rhs,
                                    start=True, stop=True)
                s_t.inc(last)
                return msk, s_t.n

            def emit_softmax(msk, qk_t, qlen, kw, first_sub=False):
                V.wait_ge(s_t.h, qk_t)
                if first_sub:
                    V.wait_ge(s_v.h, XK_V)
                sin = psS[0:qlen, 0:NH * 128].rearrange(
                    "q (h k) -> q h k", h=NH)[:, :, 0:kw]
                min_ = msk[0:qlen, 0:NH * kw].rearrange("q (h k) -> q h k", h=NH)
                pout = pbuf[0:qlen, 0:NH * kw].rearrange("q (h k) -> q h k", h=NH)
                s_v.inc(V.scalar_tensor_tensor(pout, sin, C_SCALE(qlen), min_,
                                               op0=ALU.mult, op1=ALU.add))
                S.wait_ge(s_v.h, s_v.n)
                s_s.inc(S.activation(pbuf[0:qlen, 0:NH * kw],
                                     pbuf[0:qlen, 0:NH * kw], AF.Exp))
                V.wait_ge(s_s.h, s_s.n)
                s_v.inc(V.tensor_reduce(lbuf[0:qlen, 0:NH], pout, axis=AX.X,
                                        op=ALU.add))
                s_v.inc(V.reciprocal(lbuf[0:qlen, NH:2 * NH], lbuf[0:qlen, 0:NH]))
                rb = lbuf[0:qlen, NH:2 * NH].unsqueeze(-1).broadcast_to(
                    [qlen, NH, kw])
                s_v.inc(V.tensor_tensor(pout, pout, rb, op=ALU.mult))
                return s_v.n

            def emit_transposes(pnorm_v, qlen, kw):
                nonlocal ptbuf_evac_s
                T.wait_ge(s_v.h, pnorm_v)
                per_bank = 4 if qlen == 112 else 8
                nbanks = NH // per_bank
                bank_t = []
                for b in range(nbanks):
                    for j in range(per_bank):
                        s = b * per_bank + j
                        tin = pbuf[0:qlen, kw * s: kw * s + kw]
                        col = 512 * b + qlen * j
                        last = T.transpose(psS[0:kw, col: col + qlen], tin,
                                           ident[0:qlen, 0:qlen])
                    s_t.inc(last)
                    bank_t.append(s_t.n)
                for b in range(nbanks):
                    S.wait_ge(s_t.h, bank_t[b])
                    s_s.inc(S.copy(
                        ptbuf[0:kw, per_bank * qlen * b: per_bank * qlen * (b + 1)],
                        psS[0:kw, 512 * b: 512 * b + per_bank * qlen]))
                ptbuf_evac_s = s_s.n

            def emit_pv(ss, qoff, qlen, kw, ptevac, vproj_v):
                nonlocal pv_evac_v
                T.wait_ge(s_s.h, ptevac)
                T.wait_ge(s_v.h, vproj_v)
                if pv_evac_v is not None:
                    T.wait_ge(s_s.h, pv_evac_v)
                else:
                    T.wait_ge(psPV_free[0], psPV_free[1])
                for half in range(2):
                    for h in range(8 * half, 8 * half + 8):
                        s = (h % 2) * 8 + h // 2
                        kv = h % 4
                        lhs = vbuf[0:kw, 256 * ss + 64 * kv: 256 * ss + 64 * kv + 64]
                        rhs = ptbuf[0:kw, qlen * s: qlen * s + qlen]
                        pp = 64 * (h % 2)
                        col = 128 * (h // 2)
                        last = T.matmul(psPV[pp:pp + 64, col: col + qlen], lhs, rhs,
                                        start=True, stop=True)
                    s_t.inc(last)
                    S.wait_ge(s_t.h, s_t.n)
                    pvin = psPV[:, 512 * half: 512 * half + 512].rearrange(
                        "p (c t) -> p c t", t=128)[:, :, 0:qlen]
                    aout = asub[:, 448 * half: 448 * half + 448].rearrange(
                        "p (c t) -> p c t", t=112)[:, :, 0:qlen]
                    s_s.inc(S.copy(aout, pvin))
                pv_evac_v = s_s.n

            def emit_oproj(ss, qoff, qlen, kw, *_):
                T.wait_ge(s_ws[wo_info[0][0]].h, wo_info[0][1])
                T.wait_ge(s_ws[wo_info[1][0]].h, wo_info[1][1])
                T.wait_ge(s_s.h, pv_evac_v)
                for co in range(NCH):
                    wo_v = wslot_view(wo_info[0][0] if co < 4 else wo_info[1][0])
                    bank = "A" if (co % 2 == 0) else "B"
                    ps = psA if bank == "A" else psB
                    mm_gate(bank)
                    for ci in range(NCH):
                        lhs = wo_v[:, 512 * ci + 128 * (co % 4):
                                   512 * ci + 128 * (co % 4) + 128]
                        last = T.matmul(ps[0:128, 0:qlen], lhs,
                                        asub[:, 112 * ci: 112 * ci + qlen],
                                        start=(ci == 0), stop=(ci == NCH - 1))
                    s_t.inc(last)
                    V.wait_ge(s_t.h, s_t.n)
                    xsl = xbuf[:, TT * co + HALO + qoff: TT * co + HALO + qoff + qlen]
                    ysl = xnbuf[:, TT * co + HALO + qoff: TT * co + HALO + qoff + qlen]
                    s_v.inc(V.scalar_tensor_tensor(ysl, ps[0:128, 0:qlen], C_ONE(128),
                                                   xsl, op0=ALU.mult, op1=ALU.add))
                    bank_free[bank] = (s_v.h, s_v.n)

            for ss, (qoff, qlen) in enumerate(SUBTILES):
                kw = qlen + 16
                msk, qk_t = emit_qk(ss, qoff, qlen, kw)
                if prev is not None:
                    emit_pv(*prev)
                pnorm_v = emit_softmax(msk, qk_t, qlen, kw, first_sub=(ss == 0))
                if ss == 0:
                    emit_vproj()
                    wo_info = [load_weight_half(wo3, 0, 0), load_weight_half(wo3, 1, 2)]
                if prev is not None:
                    emit_oproj(*prev)
                emit_transposes(pnorm_v, qlen, kw)
                prev = (ss, qoff, qlen, kw, ptbuf_evac_s, VPROJ_V[0])
            emit_pv(*prev)
            emit_oproj(*prev)
            YADDS_V = s_v.n
            PSS_GATE = (s_s.h, ptbuf_evac_s)
            PSPV_GATE = (s_s.h, pv_evac_v)
            wslot_free[wo_info[0][0]] = (s_t.h, s_t.n)
            wslot_free[wo_info[1][0]] = (s_t.h, s_t.n)
            wkf_info = [load_weight_half_hl("wkf", 0, 0), load_weight_half_hl("wkf", 1, 2)]
            if tt + 1 < NT:
                G.wait_ge(s_v.h, YADDS_V)
                s_x.inc(G.dma_start(
                    out=xbuf.rearrange("p (c t) -> p c t", t=TT),
                    in_=xin3[:, :, TILE * (tt + 1): TILE * (tt + 1) + TT]), 32)
                X_READY = (s_x.h, s_x.n)

            _chk(tt, "attn")

            def ffn_phase(w_infos, rhs_buf, evac, stride=512, co0=0, co1=NCH,
                          mid_cb=None, mid_at=4):
                for co in range(co0, co1):
                    if co == mid_at and mid_cb is not None:
                        mid_cb()
                    (wi, wc) = w_infos[0] if co < 4 else w_infos[1]
                    T.wait_ge(s_ws[wi].h, wc)
                    wv_ = wslot_view(wi)
                    bank = "A" if (co % 2 == 0) else "B"
                    ps = psA if bank == "A" else psB
                    mm_gate(bank)
                    for ci in range(NCH):
                        lhs = wv_[:, 512 * ci + 128 * (co % 4):
                                  512 * ci + 128 * (co % 4) + 128]
                        last = T.matmul(ps[0:128, 0:512], lhs,
                                        rhs_buf[:, stride * ci: stride * ci + 512],
                                        start=(ci == 0), stop=(ci == NCH - 1))
                    s_t.inc(last)
                    evac(co, ps, s_t.n, bank)

            def ffn_phase_hl(w_infos, rhsbf, evac, co0=0, co1=NCH,
                             mid_cb=None, mid_at=4):
                for co in range(co0, co1):
                    if co == mid_at and mid_cb is not None:
                        mid_cb()
                    (wi, wc) = w_infos[0] if co < 4 else w_infos[1]
                    T.wait_ge(s_ws[wi].h, wc)
                    wbf = wslot_view(wi).bitcast(BF16)
                    bank = "A" if (co % 2 == 0) else "B"
                    ps = psA if bank == "A" else psB
                    mm_gate(bank)
                    np_ = 3 * NCH
                    k = 0
                    for ci in range(NCH):
                        whc = wbf[:, 512 * ci + 128 * (co % 4):
                                  512 * ci + 128 * (co % 4) + 128]
                        wlc = wbf[:, 4096 + 512 * ci + 128 * (co % 4):
                                  4096 + 512 * ci + 128 * (co % 4) + 128]
                        rh = rhsbf[:, 512 * ci: 512 * ci + 512]
                        rl = rhsbf[:, 4096 + 512 * ci: 4096 + 512 * ci + 512]
                        for lhs, rhs_ in ((whc, rh), (wlc, rh), (whc, rl)):
                            last = T.matmul(ps[0:128, 0:512], lhs, rhs_,
                                            start=(k == 0), stop=(k == np_ - 1))
                            k += 1
                    s_t.inc(last)
                    evac(co, ps, s_t.n, bank)

            _chk(tt, "ffnmix")
            T.wait_ge(s_v.h, XR_V)
            S.wait_ge(s_v.h, XK_STT_V)

            def r_evac(co, ps, tcnt, bank):
                S.wait_ge(s_t.h, tcnt)
                s_s.inc(S.activation(qbuf[:, 512 * co: 512 * co + 512],
                                     ps[0:128, 0:512], AF.Sigmoid))
                bank_free[bank] = (s_s.h, s_s.n)
            ffn_phase_hl([wr_info[0], wr_info[0]], xmixbf, r_evac, co0=2, co1=4)
            wslot_free[wr_info[0][0]] = (s_t.h, s_t.n)
            wr_info[1] = load_weight_half_hl("wr", 1, 1)
            ffn_phase_hl([None, wr_info[1]], xmixbf, r_evac, co0=4)
            RSIG_LAST = s_s.n
            RMM_LAST = s_t.n
            wslot_free[wr_info[1][0]] = (s_t.h, s_t.n)
            wvf_info = [load_weight_half_hl("wvf", 0, 1), None]
            _chk(tt, "ffnr")
            T.wait_ge(s_v.h, XK_V)

            def k_evac(co, ps, tcnt, bank):
                sl = tmp2[co % 2]
                V.wait_ge(s_t.h, tcnt)
                s_v.inc(V.tensor_scalar_max(sl, ps[0:128, 0:512], C_ZERO(128)))
                bank_free[bank] = (s_v.h, s_v.n)
                S.wait_ge(s_v.h, s_v.n)
                s_s.inc(S.activation(sl, sl, AF.Square))
                V.wait_ge(s_s.h, s_s.n)
                if co == 0:
                    V.wait_ge(s_t.h, RMM_LAST)
                V.tensor_copy(xmixbf[:, 512 * co: 512 * co + 512], sl)
                V.drain()
                s_v.inc(V.tensor_tensor(
                    xmixbf[:, 4096 + 512 * co: 4096 + 512 * co + 512], sl,
                    xmixbf[:, 512 * co: 512 * co + 512], op=ALU.subtract))
            ffn_phase_hl(wkf_info, scrbf, k_evac)
            KMM_LAST = s_t.n
            if tt + 1 < NT:
                emit_x2(first=False)
            wslot_free[wkf_info[0][0]] = (s_t.h, s_t.n)
            wslot_free[wkf_info[1][0]] = (s_t.h, s_t.n)
            wvf_info[1] = load_weight_half_hl("wvf", 1, 0)
            _chk(tt, "ffnk")
            KACT_G = s_v.n
            T.wait_ge(s_v.h, KACT_G)

            gadd_n = {}

            def v_evac(co, ps, tcnt, bank):
                V.wait_ge(s_t.h, tcnt)
                tsl = pbuf[:, 512 * (co % 2): 512 * (co % 2) + 512]
                if co >= 2:
                    V.wait_ge(s_g.h, gadd_n[co - 2])
                rsrc = (rbuf2[:, 512 * co: 512 * co + 512] if co < 2
                        else qbuf[:, 512 * co: 512 * co + 512])
                s_v.inc(V.tensor_tensor(tsl, ps[0:128, 0:512], rsrc,
                                        op=ALU.mult))
                bank_free[bank] = (s_v.h, s_v.n)
                G.wait_ge(s_v.h, s_v.n)
                ysl = xnbuf[:, TT * co + HALO: TT * co + TT]
                s_g.inc(G.tensor_tensor(ysl, ysl, tsl, op=ALU.add))
                gadd_n[co] = s_g.n
                if co == 3 or co == 7:
                    hb = 0 if co == 3 else 1
                    ysem = s_y if hb == 0 else s_y1
                    ysem.inc(G.dma_start(
                        out=yout3[:, 4 * hb:4 * hb + 4, TILE * tt: TILE * tt + TILE],
                        in_=xnbuf.rearrange("p (c t) -> p c t", t=TT)
                        [:, 4 * hb:4 * hb + 4, HALO:TT]), 32)

            def stats_cb():
                if tt + 1 < NT:
                    emit_ln_stats(first=False)
            ffn_phase_hl(wvf_info, xmixbf, v_evac, mid_cb=stats_cb, mid_at=6)
            VMM_LAST = s_t.n
            VEVAC_LAST = s_v.n
            GADD_LAST = s_g.n
            wslot_free[wvf_info[0][0]] = (s_t.h, s_t.n)
            wslot_free[wvf_info[1][0]] = (s_t.h, s_t.n)
            if tt + 1 < NT:
                wkv_info = load_weight_half(None, 0, 0, kvpair=True)
                wq_info = [load_weight_half(wq3, 0, 1), load_weight_half(wq3, 1, 2)]

        except _StopBuild:
            SY.wait_ge(s_t.h, s_t.n)
            SY.wait_ge(s_v.h, s_v.n)
            SY.wait_ge(s_s.h, s_s.n)
            SY.wait_ge(s_g.h, s_g.n)
            done = s_y.n // 32
            for t2 in range(done, NT):
                s_y.inc(G.dma_start(
                    out=yout3[:, 0:4, TILE * t2: TILE * t2 + TILE],
                    in_=xbuf.rearrange("p (c t) -> p c t", t=TT)[:, 0:4, HALO:TT]), 32)
                s_y1.inc(G.dma_start(
                    out=yout3[:, 4:8, TILE * t2: TILE * t2 + TILE],
                    in_=xbuf.rearrange("p (c t) -> p c t", t=TT)[:, 4:8, HALO:TT]), 32)

        G.wait_ge(s_y.h, 32 * NT)
        G.wait_ge(s_y1.h, 32 * NT)

        block.sync(SY._flush)
        block.tensor(T._flush)
        block.vector(V._flush)
        block.scalar(S._flush)
        block.gpsimd(G._flush)

    return nc


_NC_CACHE = [None]


def kernel(L0_attn_x, L0_attn_wq, L0_attn_wk, L0_attn_wv, L0_attn_wo,
           L0_attn_q_norm_w, L0_attn_k_norm_w, L0_ffn_W_r, L0_ffn_W_k,
           L0_ffn_W_v, L0_ffn_mu_r, L0_ffn_mu_k):
    x = np.asarray(L0_attn_x, dtype=np.float32)
    f32c = lambda a: np.ascontiguousarray(np.asarray(a, dtype=np.float32))
    wq_t = f32c(np.asarray(L0_attn_wq).T)
    wk_t = f32c(np.asarray(L0_attn_wk).T)
    wv_t = f32c(np.asarray(L0_attn_wv).T)
    wo_t = f32c(np.asarray(L0_attn_wo).T)
    wr_t = f32c(np.asarray(L0_ffn_W_r).T)
    wkf_t = f32c(np.asarray(L0_ffn_W_k).T)
    wvf_t = f32c(np.asarray(L0_ffn_W_v).T)
    qnw = f32c(np.tile(np.asarray(L0_attn_q_norm_w), 2)[:, None])
    knw = f32c(np.tile(np.asarray(L0_attn_k_norm_w), 2)[:, None])
    mur = f32c(np.asarray(L0_ffn_mu_r).reshape(NCH, 128).T)
    muk = f32c(np.asarray(L0_ffn_mu_k).reshape(NCH, 128).T)
    scal = f32c(np.tile(np.array([0.125, 1e-5, 0.01, -1.0, 1.0, 0.0, 0.0, 0.0],
                                 dtype=np.float32), (128, 1)))
    cn = _consts()

    in_maps = []
    for c in range(8):
        b, h = c // 2, c % 2
        xt = x[b].T
        if h == 0:
            xcore = np.concatenate(
                [np.zeros((D, HALO), dtype=np.float32), xt[:, :CORE_T]], axis=1)
            lmf = cn["logmf"]
        else:
            xcore = xt[:, CORE_T - HALO: 2 * CORE_T]
            lmf = cn["logm"]
        in_maps.append({
            "xin": np.ascontiguousarray(xcore),
            "w_wq": wq_t, "w_wk": wk_t, "w_wv": wv_t, "w_wo": wo_t,
            "w_wr": wr_t, "w_wkf": wkf_t, "w_wvf": wvf_t,
            "c_ident": cn["ident"], "c_logm": cn["logm"], "c_logmf": lmf,
            "c_logm64": cn["logm64"], "c_selq": cn["selq"], "c_selqT": cn["selqT"],
            "c_selk": cn["selk"], "c_selkT": cn["selkT"],
            "c_qnw": qnw, "c_knw": knw, "c_mur": mur, "c_muk": muk, "c_scal": scal,
        })

    if _NC_CACHE[0] is None:
        _NC_CACHE[0] = build_program()
    nc = _NC_CACHE[0]
    res = run_bass_kernel_spmd(nc, in_maps, core_ids=list(range(8)))

    y = np.empty((B, L, D), dtype=np.float32)
    for c in range(8):
        b, h = c // 2, c % 2
        y[b, h * CORE_T:(h + 1) * CORE_T, :] = res.results[c]["yout"].T
    return y
